# version 28
# baseline (speedup 1.0000x reference)
import math
from contextlib import ExitStack

import numpy as np
import concourse.bass as bass
import concourse.mybir as mybir
from concourse.bass_utils import run_bass_kernel_spmd

F32 = mybir.dt.float32
BF16 = mybir.dt.bfloat16
AF = mybir.ActivationFunctionType
ALU = mybir.AluOpType
AX = mybir.AxisListType

T = 4096
D = 1024
ENGS = ("tensor", "vector", "scalar", "gpsimd", "sync")

MAGIC = 12582912.0


class Sem:
    __slots__ = ("name", "handle", "count")

    def __init__(self, name):
        self.name = name
        self.handle = None
        self.count = 0


class VP:

    def __init__(self, nc):
        self.nc = nc
        self.thunks = {e: [] for e in ENGS}
        self.sems = []
        self.selfs = {e: self.sem("self_" + e) for e in ENGS}

    def sem(self, name):
        s = Sem(name)
        self.sems.append(s)
        return s

    def on(self, eng, fn, waits=(), inc=None, amt=1, sync=False):
        flat = []
        for w in waits:
            if not w:
                continue
            if isinstance(w, list):
                flat.extend(x for x in w if x)
            else:
                flat.append(w)
        waits = tuple(flat)
        if sync and inc is None:
            inc = self.selfs[eng]
        stok = inc.count + amt if (sync and inc is not None) else 0

        def thunk(h, fn=fn, waits=waits, inc=inc, amt=amt, sync=sync, stok=stok):
            for (s, v) in waits:
                if v > 0:
                    h.wait_ge(s.handle, v)
            ins = fn(h)
            if inc is not None:
                ins.then_inc(inc.handle, amt)
            if sync:
                h.wait_ge(inc.handle, stok)

        self.thunks[eng].append(thunk)
        if inc is not None:
            inc.count += amt
            return inc.count
        return None

    def wait(self, eng, sem, val):
        def thunk(h, sem=sem, val=val):
            if val > 0:
                h.wait_ge(sem.handle, val)

        self.thunks[eng].append(thunk)

    def barrier(self, bar):
        target = bar.count + len(ENGS)
        for e in ENGS:
            def thunk(h, bar=bar, target=target):
                h.sem_inc(bar.handle, 1)
                h.wait_ge(bar.handle, target)

            self.thunks[e].append(thunk)
        bar.count = target

    def run(self):
        nc = self.nc
        with ExitStack() as stk:
            for s in self.sems:
                s.handle = stk.enter_context(nc.semaphore(s.name))
            with nc.Block() as blk:
                def runner(eng):
                    def go(h, eng=eng):
                        for t in self.thunks[eng]:
                            t(h)
                    return go

                blk.tensor(runner("tensor"))
                blk.vector(runner("vector"))
                blk.scalar(runner("scalar"))
                blk.gpsimd(runner("gpsimd"))
                blk.sync(runner("sync"))


class PsRot:

    def __init__(self, banks):
        self.banks = banks
        self.free = [None] * len(banks)
        self.i = 0

    def next(self):
        i = self.i % len(self.banks)
        self.i += 1
        return self.banks[i], self.free[i], i

    def set_free(self, i, sem, tok):
        self.free[i] = (sem, tok)

    def set_free2(self, i, *gates):
        self.free[i] = list(gates)


def make_identity(vp, ident, inc=None):
    vp.on("gpsimd", lambda h, a=ident: h.memset(a[:, :], 0.0))
    n = ident.shape[0]
    return vp.on(
        "gpsimd",
        lambda h, a=ident, n=n: h.affine_select(
            out=a[:, :], in_=a[:, :], compare_op=ALU.not_equal, fill=1.0,
            base=0, pattern=[[-1, n]], channel_multiplier=1),
        inc=inc,
    )


def layernorm_inplace(vp, s, x_tile, sqb, width=D, eps=1e-5):
    vp.on("vector", lambda h: h.reduce_sum(s.col0[:, 0:1], x_tile, axis=AX.X),
          sync=True)
    vp.on("vector", lambda h: h.tensor_scalar(s.col0[:, 0:1], s.col0[:, 0:1],
                                              -1.0 / width, None, op0=ALU.mult),
          sync=True)
    vp.on("vector", lambda h: h.tensor_scalar(x_tile, x_tile, s.col0[:, 0:1], None,
                                              op0=ALU.add), sync=True)
    vp.on("vector", lambda h: h.tensor_tensor(sqb[:, 0:width], x_tile, x_tile,
                                              op=ALU.mult), sync=True)
    vp.on("vector", lambda h: h.reduce_sum(s.col1[:, 0:1], sqb[:, 0:width], axis=AX.X),
          sync=True)
    tok = vp.on("vector", lambda h: h.tensor_scalar(s.col1[:, 0:1], s.col1[:, 0:1],
                                                    1.0 / width, eps, op0=ALU.mult,
                                                    op1=ALU.add), inc=s.dve, sync=True)
    vp.on("scalar", lambda h: h.activation(s.col2[:, 0:1], s.col1[:, 0:1], AF.Ln),
          waits=[(s.dve, tok)], sync=True)
    tok3 = vp.on("scalar", lambda h: h.activation(s.col2[:, 0:1], s.col2[:, 0:1],
                                                  AF.Exp, scale=-0.5), inc=s.act,
                 sync=True)
    tok4 = vp.on("vector", lambda h: h.tensor_scalar(x_tile, x_tile, s.col2[:, 0:1],
                                                     None, op0=ALU.mult),
                 waits=[(s.act, tok3)], inc=s.dve)
    return tok4


class NS:
    pass


def mk_sems(vp, pfx):
    s = NS()
    s.dld = vp.sem(pfx + "_dld")
    s.dst = vp.sem(pfx + "_dst")
    s.pe = vp.sem(pfx + "_pe")
    s.dve = vp.sem(pfx + "_dve")
    s.act = vp.sem(pfx + "_act")
    s.pool = vp.sem(pfx + "_pool")
    s.bar = vp.sem(pfx + "_bar")
    return s



def ternarize_and_transpose(vp, s, w_hbm, rows, cols, wstages, wT_dst,
                            ident_bf, tr_rot, wst_free):
    n_rt = rows // 128
    n_ct = cols // 128
    for rt in range(n_rt):
        wstage = wstages[rt % 2]
        tok_ld = vp.on("sync",
                       lambda h, rt=rt, wstage=wstage: h.dma_start(
                           out=wstage[:, 0:cols],
                           in_=w_hbm[rt * 128:(rt + 1) * 128, :]),
                       waits=[wst_free[rt % 2]],
                       inc=s.dld, amt=16)
        vp.on("vector",
              lambda h, wstage=wstage: h.reduce_sum(
                  s.col0[:, 0:1], wstage[:, 0:cols], axis=AX.X,
                  apply_absolute_value=True),
              waits=[(s.dld, tok_ld)], sync=True)
        vp.on("vector", lambda h: h.tensor_scalar(s.col1[:, 0:1], s.col0[:, 0:1],
                                                  1.0 / cols, 1e-5, op0=ALU.mult,
                                                  op1=ALU.max), sync=True)
        vp.on("vector", lambda h: h.reciprocal(s.col2[:, 0:1], s.col1[:, 0:1]),
              sync=True)
        vp.on("vector", lambda h, wstage=wstage: h.tensor_scalar(
            wstage[:, 0:cols], wstage[:, 0:cols], s.col2[:, 0:1], None,
            op0=ALU.mult), sync=True)
        vp.on("vector", lambda h, wstage=wstage: h.tensor_scalar(
            wstage[:, 0:cols], wstage[:, 0:cols], MAGIC, MAGIC, op0=ALU.add,
            op1=ALU.subtract), sync=True)
        vp.on("vector", lambda h, wstage=wstage: h.tensor_scalar(
            wstage[:, 0:cols], wstage[:, 0:cols], 1.0, -1.0, op0=ALU.min,
            op1=ALU.max), sync=True)
        tok_w = vp.on("vector",
                      lambda h, wstage=wstage: h.tensor_scalar(
                          wstage[:, 0:cols], wstage[:, 0:cols],
                          s.col1[:, 0:1], None, op0=ALU.mult),
                      inc=s.dve)
        for ct in range(n_ct):
            bank, fr, bi = tr_rot.next()
            tok_tr = vp.on("tensor",
                           lambda h, ct=ct, bank=bank, wstage=wstage: h.transpose(
                               bank[:, :], wstage[:, ct * 128:(ct + 1) * 128],
                               ident_bf[:, :]),
                           waits=[(s.dve, tok_w), fr], inc=s.pe)
            tok_e = vp.on("vector",
                          lambda h, ct=ct, rt=rt, bank=bank: h.tensor_copy(
                              wT_dst[:, ct * rows + rt * 128: ct * rows + (rt + 1) * 128],
                              bank[:, :]),
                          waits=[(s.pe, tok_tr)], inc=s.dve)
            tr_rot.set_free(bi, s.dve, tok_e)
        wst_free[rt % 2] = (s.pe, s.pe.count)


def build_phase_b(nc, vp, io):
    s = mk_sems(vp, "b")

    stk = ExitStack()
    sb = lambda name, shape, dt: stk.enter_context(nc.sbuf_tensor(name, shape, dt))
    psb = lambda name, shape, dt=F32: stk.enter_context(nc.psum_tensor(name, shape, dt))

    wfcT = sb("b_wfcT", [128, 8 * 4096], BF16)
    wprojT = sb("b_wprojT", [128, 32 * 1024], BF16)
    vT = sb("b_vT", [128, 8 * 16], BF16)
    uT = sb("b_uT", [16, 4096], BF16)
    fbTg = sb("b_fbTg", [16, 16], BF16)
    ident32b = sb("b_ident32b", [128, 128], F32)
    ones_row = sb("b_ones", [1, 16], F32)
    s.col0 = sb("b_col0", [128, 1], F32)
    s.col1 = sb("b_col1", [128, 1], F32)
    s.col2 = sb("b_col2", [128, 1], F32)
    ps_tr1 = psb("b_pstr1", [128, 128], F32)
    ps_tr2 = psb("b_pstr2", [128, 128], F32)
    ps_a = psb("b_psa", [128, 512])
    ps_b = psb("b_psb", [128, 512])
    ps_c = psb("b_psc", [128, 512])
    ps_d = psb("b_psd", [128, 512])
    ps_s1 = psb("b_pss1", [16, 512])
    tr_rot = PsRot([ps_tr1, ps_tr2])
    h_rot = PsRot([ps_a, ps_b])
    y_rot = PsRot([ps_c, ps_d])

    make_identity(vp, ident32b, inc=s.pool)
    vp.on("gpsimd", lambda h: h.memset(ones_row[:, :], 1.0), inc=s.pool)

    vp.wait("tensor", s.pool, 1)

    with nc.sbuf_tensor("b_wstage", [128, 4096], F32) as wstage, \
         nc.sbuf_tensor("b_wstage2", [128, 4096], F32) as wstage2:
        wsts = [wstage, wstage2]
        wstf = [None, None]
        ternarize_and_transpose(vp, s, io["w_fc"], 4096, 1024, wsts,
                                wfcT, ident32b, tr_rot, wstf)
        ternarize_and_transpose(vp, s, io["w_proj"], 1024, 4096, wsts,
                                wprojT, ident32b, tr_rot, wstf)

        tok = vp.on("sync", lambda h: h.dma_start(out=wstage[0:16, 0:1024],
                                                  in_=io["v"]),
                    waits=[(s.pe, s.pe.count)], inc=s.dld, amt=16)
        for j in range(8):
            bank, fr, bi = tr_rot.next()
            tok_tr = vp.on("tensor",
                           lambda h, j=j, bank=bank: h.transpose(
                               bank[:, 0:16], wstage[0:16, j * 128:(j + 1) * 128],
                               ident32b[0:16, 0:16]),
                           waits=[(s.dld, tok), fr], inc=s.pe)
            tok_e = vp.on("vector",
                          lambda h, j=j, bank=bank: h.tensor_copy(
                              vT[:, j * 16:(j + 1) * 16], bank[:, 0:16]),
                          waits=[(s.pe, tok_tr)], inc=s.dve)
            tr_rot.set_free(bi, s.dve, tok_e)
        tok = vp.on("sync", lambda h: h.dma_start(
            out=wstage2[:, 0:512].rearrange("p (rt r) -> p rt r", r=16),
            in_=io["u"].rearrange("(rt p) r -> p rt r", p=128)),
                    waits=[(s.pe, s.pe.count)], inc=s.dld, amt=16)
        for rt in range(32):
            bank, fr, bi = tr_rot.next()
            tok_tr = vp.on("tensor",
                           lambda h, rt=rt, bank=bank: h.transpose(
                               bank[0:16, 0:128], wstage2[:, rt * 16:(rt + 1) * 16],
                               ident32b[:, :]),
                           waits=[(s.dld, tok), fr], inc=s.pe)
            tok_e = vp.on("vector",
                          lambda h, rt=rt, bank=bank: h.tensor_copy(
                              uT[0:16, rt * 128:(rt + 1) * 128], bank[0:16, 0:128]),
                          waits=[(s.pe, tok_tr)], inc=s.dve)
            tr_rot.set_free(bi, s.dve, tok_e)
        tok = vp.on("sync", lambda h: h.dma_start(out=wstage[0:16, 0:16],
                                                  in_=io["fast_b"]),
                    waits=[(s.pe, s.pe.count)], inc=s.dld, amt=16)
        tokg = vp.on("sync", lambda h: h.dma_start(out=wstage[0:1, 16:17],
                                                   in_=io["gate"]),
                     inc=s.dld, amt=16)
        tok_g = vp.on("tensor", lambda h: h.matmul(ps_s1[:, 0:1], ones_row[0:1, 0:16],
                                                   wstage[0:1, 16:17], start=True,
                                                   stop=True),
                      waits=[(s.dld, tokg), (s.pool, 2)], inc=s.pe)
        tok_gc = vp.on("vector", lambda h: h.tensor_copy(s.col0[0:16, 0:1],
                                                         ps_s1[0:16, 0:1]),
                       waits=[(s.pe, tok_g)], inc=s.dve)
        tok_tr = vp.on("tensor", lambda h: h.transpose(ps_s1[0:16, 0:16],
                                                       wstage[0:16, 0:16],
                                                       ident32b[0:16, 0:16]),
                       waits=[(s.dld, tok), (s.dve, tok_gc)], inc=s.pe)
        vp.on("vector", lambda h: h.tensor_scalar(fbTg[0:16, 0:16], ps_s1[0:16, 0:16],
                                                  s.col0[0:16, 0:1], None, op0=ALU.mult),
              waits=[(s.pe, tok_tr)], inc=s.dve)

        vp.barrier(s.bar)

    def dbg(name, ap_src):
        if name in io:
            vp.on("gpsimd", lambda h, name=name, ap_src=ap_src: h.dma_start(
                out=io[name], in_=ap_src),
                waits=[(s.dve, s.dve.count), (s.act, s.act.count)],
                inc=s.dst, amt=16)

    dbg("dbg_wfcT", wfcT[:, :])
    dbg("dbg_wprojT", wprojT[:, :])
    dbg("dbg_vT", vT[:, :])
    dbg("dbg_uT", uT[:, :])
    dbg("dbg_fbTg", fbTg[:, :])

    sc = sb("b_sc", [128, 4 * D], F32)
    sc2 = sc
    xTb = sb("b_xTb", [128, 8 * 512], BF16)
    l1b = sb("b_l1b", [16, 512], BF16)
    l2b = sb("b_l2b", [16, 512], BF16)
    rbuf = sb("b_rbuf", [128, 512], BF16)
    r2sb = sb("b_r2", [128, 32 * 512], BF16)
    sqb = xTb.ap().bitcast(F32)

    s1r = io["s1"].rearrange("(tb tt p) d -> tb p tt d", p=128, tt=4)
    s2r = io["s2"].rearrange("(tb tt p) d -> tb p tt d", p=128, tt=4)

    sc_store_tok = [0, 0]
    for tb in range(8):
        sc_t = sc
        tok_ld = vp.on("sync",
                       lambda h, tb=tb, sc_t=sc_t: h.dma_start(
                           out=sc_t.ap().rearrange("p (tt d) -> p tt d", tt=4),
                           in_=s1r[tb]),
                       waits=[(s.dst, max(sc_store_tok))],
                       inc=s.dld, amt=16)
        for tt in range(4):
            for j in range(8):
                bank, fr, bi = tr_rot.next()
                tok_tr = vp.on("tensor",
                               lambda h, tt=tt, j=j, bank=bank, sc_t=sc_t:
                               h.transpose(
                                   bank[:, :],
                                   sc_t[:, tt * D + j * 128:tt * D + (j + 1) * 128],
                                   ident32b[:, :]),
                               waits=[(s.dld, tok_ld), fr], inc=s.pe)
                tok_e = vp.on("vector",
                              lambda h, tt=tt, j=j, bank=bank: h.tensor_copy(
                                  xTb[:, j * 512 + tt * 128:j * 512 + (tt + 1) * 128],
                                  bank[:, :]),
                              waits=[(s.pe, tok_tr)], inc=s.dve)
                tr_rot.set_free(bi, s.dve, tok_e)
        tok_xT = s.dve.count

        for kc in range(8):
            vp.on("tensor",
                  lambda h, kc=kc: h.matmul(ps_s1[:, :], vT[:, kc * 16:(kc + 1) * 16],
                                            xTb[:, kc * 512:(kc + 1) * 512],
                                            start=(kc == 0), stop=(kc == 7)),
                  waits=[(s.dve, tok_xT)] if kc == 0 else (),
                  inc=s.pe if kc == 7 else None)
        tok_l1 = vp.on("vector", lambda h: h.tensor_copy(l1b[:, :], ps_s1[0:16, :]),
                       waits=[(s.pe, s.pe.count)], inc=s.dve)
        vp.on("tensor", lambda h: h.matmul(ps_s1[0:16, 0:512], fbTg[:, :], l1b[:, :],
                                           start=True, stop=True),
              waits=[(s.dve, tok_l1)], inc=s.pe)
        tok_l2 = vp.on("vector", lambda h: h.tensor_copy(l2b[:, :], ps_s1[0:16, :]),
                       waits=[(s.pe, s.pe.count)], inc=s.dve)

        for hc in range(32):
            bank, fr, bi = h_rot.next()
            for kc in range(8):
                vp.on("tensor",
                      lambda h, hc=hc, kc=kc, bank=bank: h.matmul(
                          bank[:, :],
                          wfcT[:, kc * 4096 + hc * 128:kc * 4096 + (hc + 1) * 128],
                          xTb[:, kc * 512:(kc + 1) * 512],
                          start=(kc == 0), stop=False),
                      waits=([(s.dve, tok_l2), fr] if kc == 0 else ()))
            tok_h = vp.on("tensor",
                          lambda h, hc=hc, bank=bank: h.matmul(
                              bank[:, :], uT[:, hc * 128:(hc + 1) * 128], l2b[:, :],
                              start=False, stop=True),
                          inc=s.pe)
            tok_r = vp.on("scalar",
                          lambda h, bank=bank: h.activation(rbuf[:, :], bank[:, :],
                                                            AF.Relu),
                          waits=[(s.pe, tok_h), (s.dve, s.dve.count)], inc=s.act)
            h_rot.set_free(bi, s.act, tok_r)
            vp.on("vector",
                  lambda h, hc=hc: h.tensor_tensor(r2sb[:, hc * 512:(hc + 1) * 512],
                                                   rbuf[:, :], rbuf[:, :], op=ALU.mult),
                  waits=[(s.act, tok_r)], inc=s.dve)

        for tt in range(4):
            for nh in range(2):
                bank, fr, bi = y_rot.next()
                for hc in range(32):
                    vp.on("tensor",
                          lambda h, tt=tt, nh=nh, hc=hc, bank=bank: h.matmul(
                              bank[:, :],
                              r2sb[:, hc * 512 + tt * 128:hc * 512 + (tt + 1) * 128],
                              wprojT[:, hc * 1024 + nh * 512:hc * 1024 + (nh + 1) * 512],
                              start=(hc == 0), stop=(hc == 31)),
                          waits=([(s.dve, s.dve.count), fr] if hc == 0 else ()),
                          inc=s.pe if hc == 31 else None)
                tok_y = s.pe.count
                tok_res = vp.on("vector",
                                lambda h, tt=tt, nh=nh, bank=bank, sc_t=sc_t:
                                h.tensor_tensor(
                                    sc_t[:, tt * D + nh * 512:tt * D + (nh + 1) * 512],
                                    bank[:, :],
                                    sc_t[:, tt * D + nh * 512:tt * D + (nh + 1) * 512],
                                    op=ALU.add),
                                waits=[(s.pe, tok_y)], inc=s.dve)
                y_rot.set_free(bi, s.dve, tok_res)

        if tb == 0:
            dbg("dbg_xTb", xTb[:, :])
            dbg("dbg_l1b", l1b[:, :])
            dbg("dbg_l2b", l2b[:, :])
            dbg("dbg_r2", r2sb[:, :])
            dbg("dbg_res", sc[:, :])

        for tt in range(4):
            tok_ln = layernorm_inplace(vp, s, sc_t[:, tt * D:(tt + 1) * D], sqb)
        sc_store_tok[tb % 2] = vp.on(
            "gpsimd",
            lambda h, tb=tb, sc_t=sc_t: h.dma_start(
                out=s2r[tb],
                in_=sc_t.ap().rearrange("p (tt d) -> p tt d", tt=4)),
            waits=[(s.dve, tok_ln)], inc=s.dst, amt=16)

    vp.wait("gpsimd", s.dst, s.dst.count)
    return stk




def transpose_weight(vp, s, w_hbm, rows, cols, wstages, weffs, dst, ident_bf,
                     tr_rot, wst_free, weff_free):
    n_rt = rows // 128
    n_ct = cols // 128
    for rt in range(n_rt):
        wstage = wstages[rt % 2]
        weff = weffs[rt % 2]
        tok_ld = vp.on("sync",
                       lambda h, rt=rt, wstage=wstage: h.dma_start(
                           out=wstage[:, 0:cols],
                           in_=w_hbm[rt * 128:(rt + 1) * 128, :]),
                       waits=[wst_free[rt % 2]],
                       inc=s.dld, amt=16)
        tok_w = vp.on("vector",
                      lambda h, wstage=wstage, weff=weff: h.tensor_copy(
                          weff[:, 0:cols], wstage[:, 0:cols]),
                      waits=[(s.dld, tok_ld), weff_free[rt % 2]],
                      inc=s.dve)
        wst_free[rt % 2] = (s.dve, tok_w)
        for ct in range(n_ct):
            bank, fr, bi = tr_rot.next()
            bb = bank.ap().bitcast(BF16)[:, 0:128]
            tok_tr = vp.on("tensor",
                           lambda h, ct=ct, bb=bb, weff=weff: h.transpose(
                               bb, weff[:, ct * 128:(ct + 1) * 128],
                               ident_bf[:, :]),
                           waits=[(s.dve, tok_w), fr], inc=s.pe)
            tok_e = vp.on("vector",
                          lambda h, ct=ct, rt=rt, bb=bb: h.tensor_copy(
                              dst[:, ct * rows + rt * 128: ct * rows + (rt + 1) * 128],
                              bb),
                          waits=[(s.pe, tok_tr)], inc=s.dve)
            tr_rot.set_free(bi, s.dve, tok_e)
        weff_free[rt % 2] = (s.pe, s.pe.count)


def build_phase_a(nc, vp, io):
    s = mk_sems(vp, "a")
    H, DH = 16, 64
    C = 512
    NCH = T // C

    stk = ExitStack()
    sb = lambda name, shape, dt: stk.enter_context(nc.sbuf_tensor(name, shape, dt))
    psb = lambda name, shape, dt=F32: stk.enter_context(nc.psum_tensor(name, shape, dt))

    wqT = sb("a_wqT", [128, 8 * 1024], BF16)
    wkT = sb("a_wkT", [128, 8 * 1024], BF16)
    wvT = sb("a_wvT", [128, 8 * 1024], BF16)
    woT = sb("a_woT", [128, 8 * 1024], BF16)
    bqcol = sb("a_bqcol", [64, 16], F32)
    bkcol = sb("a_bkcol", [64, 16], F32)
    nbqcol = sb("a_nbqcol", [64, 16], F32)
    nbkcol = sb("a_nbkcol", [64, 16], F32)
    bv_bc = sb("a_bvbc", [128, 1024], F32)
    bo_bc = sb("a_bobc", [128, 1024], F32)
    identb = sb("a_identb", [128, 128], BF16)
    ident32a = sb("a_ident32a", [128, 128], F32)
    ones_row = sb("a_ones", [1, 128], F32)
    masks = sb("a_masks", [128, 4 * 512], F32)
    Sz = sb("a_Sz", [64, H * 65], F32)
    Szb = sb("a_Szb", [64, H * 65], BF16)
    s.col0 = sb("a_col0", [128, 1], F32)
    s.col1 = sb("a_col1", [128, 1], F32)
    s.col2 = sb("a_col2", [128, 1], F32)


    ps_tr1 = psb("a_pstr1", [128, 128], F32)
    ps_tr2 = psb("a_pstr2", [128, 128], F32)
    ps_a = psb("a_psa", [128, 512])
    ps_b = psb("a_psb", [128, 512])
    ps_n1 = psb("a_psn1", [128, 65])
    ps_n2 = psb("a_psn2", [128, 65])
    ps_st = psb("a_psst", [64, 65])
    ps_c2 = psb("a_psc2", [128, 512])
    tr_rot = PsRot([ps_tr1, ps_tr2])
    big_rot = PsRot([ps_a, ps_b, ps_c2])
    nd_rot = PsRot([ps_n1, ps_n2])

    make_identity(vp, identb)
    make_identity(vp, ident32a)
    vp.on("gpsimd", lambda h: h.memset(ones_row[:, :], 1.0))
    vp.on("gpsimd", lambda h: h.memset(Sz[:, :], 0.0))
    vp.on("gpsimd", lambda h: h.memset(Szb[:, :], 0.0))
    vp.on("gpsimd", lambda h: h.memset(vz[:, :], 1.0))
    vp.on("gpsimd", lambda h: h.memset(vz2[:, :], 1.0))
    for cki in range(4):
        vp.on("gpsimd", lambda h, cki=cki: h.memset(masks[:, cki * 512:(cki + 1) * 512], 1.0))
        vp.on("gpsimd",
              lambda h, cki=cki: h.affine_select(
                  out=masks[:, cki * 512:(cki + 1) * 512],
                  in_=masks[:, cki * 512:(cki + 1) * 512],
                  compare_op=ALU.is_ge, fill=0.0, base=-cki * 128,
                  pattern=[[1, 512]], channel_multiplier=-1),
              inc=None)
    vp.on("gpsimd", lambda h: h.engine_nop(), inc=s.pool)

    vp.wait("tensor", s.pool, 1)

    with nc.sbuf_tensor("a_wstage", [128, 1024], F32) as wstage, \
         nc.sbuf_tensor("a_wstage2", [128, 1024], F32) as wstage2, \
         nc.sbuf_tensor("a_weff", [128, 1024], BF16) as weff, \
         nc.sbuf_tensor("a_weff2", [128, 1024], BF16) as weff2:
        wsts, wefs = [wstage, wstage2], [weff, weff2]
        wstf, weft = [None, None], [None, None]
        transpose_weight(vp, s, io["Wq"], 1024, 1024, wsts, wefs, wqT, identb,
                         tr_rot, wstf, weft)
        transpose_weight(vp, s, io["Wk"], 1024, 1024, wsts, wefs, wkT, identb,
                         tr_rot, wstf, weft)
        transpose_weight(vp, s, io["Wv"], 1024, 1024, wsts, wefs, wvT, identb,
                         tr_rot, wstf, weft)
        transpose_weight(vp, s, io["Wo"], 1024, 1024, wsts, wefs, woT, identb,
                         tr_rot, wstf, weft)

        for h_ in range(16):
            vp.on("sync",
                  lambda hh, h_=h_: hh.dma_start(
                      out=bqcol[:, h_:h_ + 1],
                      in_=io["bq"].rearrange("(a b) -> a b", b=1)[h_ * 64:(h_ + 1) * 64]),
                  inc=s.dld, amt=16)
            vp.on("sync",
                  lambda hh, h_=h_: hh.dma_start(
                      out=bkcol[:, h_:h_ + 1],
                      in_=io["bk"].rearrange("(a b) -> a b", b=1)[h_ * 64:(h_ + 1) * 64]),
                  inc=s.dld, amt=16)
        tok_bv = vp.on("sync", lambda h: h.dma_start(
            out=wstage[0:1, 0:1024], in_=io["bv"].rearrange("(a d) -> a d", a=1)),
            waits=[(s.dve, s.dve.count)], inc=s.dld, amt=16)

        vp.on("vector", lambda h: h.tensor_scalar(nbqcol[:, :], bqcol[:, :], -1.0,
                                                  None, op0=ALU.mult),
              waits=[(s.dld, s.dld.count)])
        vp.on("vector", lambda h: h.tensor_scalar(nbkcol[:, :], bkcol[:, :], -1.0,
                                                  None, op0=ALU.mult), inc=s.dve)
        for half in range(2):
            bank, fr, bi = big_rot.next()
            tk = vp.on("tensor",
                       lambda h, half=half, bank=bank: h.matmul(
                           bank[:, :], ones_row[0:1, :],
                           wstage[0:1, half * 512:(half + 1) * 512],
                           start=True, stop=True),
                       waits=[(s.dld, tok_bv), fr], inc=s.pe)
            te = vp.on("vector",
                       lambda h, half=half, bank=bank: h.tensor_copy(
                           bv_bc[:, half * 512:(half + 1) * 512], bank[:, :]),
                       waits=[(s.pe, tk)], inc=s.dve)
            big_rot.set_free(bi, s.dve, te)
        tok_bo = vp.on("sync", lambda h: h.dma_start(
            out=wstage[0:1, 0:1024], in_=io["bo"].rearrange("(a d) -> a d", a=1)),
            waits=[(s.pe, s.pe.count)], inc=s.dld, amt=16)
        for half in range(2):
            bank, fr, bi = big_rot.next()
            tk = vp.on("tensor",
                       lambda h, half=half, bank=bank: h.matmul(
                           bank[:, :], ones_row[0:1, :],
                           wstage[0:1, half * 512:(half + 1) * 512],
                           start=True, stop=True),
                       waits=[(s.dld, tok_bo), fr], inc=s.pe)
            te = vp.on("vector",
                       lambda h, half=half, bank=bank: h.tensor_copy(
                           bo_bc[:, half * 512:(half + 1) * 512], bank[:, :]),
                       waits=[(s.pe, tk)], inc=s.dve)
            big_rot.set_free(bi, s.dve, te)

        vp.barrier(s.bar)

    xc = sb("a_xc", [128, 4 * 1024], F32)
    xc2 = sb("a_xc2", [128, 4 * 1024], F32)
    xTc = sb("a_xTc", [128, 8 * 512], BF16)
    fq64 = sb("a_fq64", [64, H * 512], BF16)
    fk64 = sb("a_fk64", [64, H * 512], BF16)
    ev_r = sb("a_evr", [64, 512], F32)
    ev_m = sb("a_evm", [64, 512], F32)
    ev_e = sb("a_eve", [64, 512], F32)
    v_c = sb("a_vc", [128, 4 * 1024], BF16)
    vz = sb("a_vz", [128, 4 * 65], BF16)
    vz2 = sb("a_vz2", [128, 4 * 65], BF16)
    ktm = sb("a_ktm", [128, 4 * 64], BF16)
    ktm2 = sb("a_ktm2", [128, 4 * 64], BF16)
    AT = sb("a_AT", [128, 4 * 512], BF16)
    AT2 = sb("a_AT2", [128, 4 * 512], BF16)
    att = sb("a_att", [128, 4 * 1024], BF16)
    attT = sb("a_attT", [128, 8 * 512], BF16)
    sqb = sb("a_sqb", [128, D], F32)

    xr = io["x"].rearrange("(cc tt p) d -> cc p tt d", p=128, tt=4)
    s1r = io["s1"].rearrange("(cc tt p) d -> cc p tt d", p=128, tt=4)

    xc_store_tok = [0, 0]
    for cc in range(NCH):
        xc_c = xc if cc % 2 == 0 else xc2
        tok_ld = vp.on("sync",
                       lambda h, cc=cc, xc_c=xc_c: h.dma_start(
                           out=xc_c.ap().rearrange("p (tt d) -> p tt d", tt=4),
                           in_=xr[cc]),
                       waits=[(s.dst, xc_store_tok[cc % 2])],
                       inc=s.dld, amt=16)
        for tt in range(4):
            for j in range(8):
                bank, fr, bi = tr_rot.next()
                tok_tr = vp.on("tensor",
                               lambda h, tt=tt, j=j, bank=bank, xc_c=xc_c:
                               h.transpose(
                                   bank[:, :],
                                   xc_c[:, tt * D + j * 128:tt * D + (j + 1) * 128],
                                   ident32a[:, :]),
                               waits=[(s.dld, tok_ld), fr], inc=s.pe)
                tok_e = vp.on("vector",
                              lambda h, tt=tt, j=j, bank=bank: h.tensor_copy(
                                  xTc[:, j * 512 + tt * 128:j * 512 + (tt + 1) * 128],
                                  bank[:, :]),
                              waits=[(s.pe, tok_tr)], inc=s.dve)
                tr_rot.set_free(bi, s.dve, tok_e)
        tok_xT = s.dve.count

        for (wT, bcol, nbcol, fdst) in ((wqT, bqcol, nbqcol, fq64),
                                        (wkT, bkcol, nbkcol, fk64)):
            for h_ in range(16):
                bank, fr, bi = big_rot.next()
                for kc in range(8):
                    vp.on("tensor",
                          lambda h, h_=h_, kc=kc, bank=bank, wT=wT: h.matmul(
                              bank[0:64, :],
                              wT[:, kc * 1024 + h_ * 64:kc * 1024 + (h_ + 1) * 64],
                              xTc[:, kc * 512:(kc + 1) * 512],
                              start=(kc == 0), stop=(kc == 7)),
                          waits=([(s.dve, tok_xT), fr] if kc == 0 else ()),
                          inc=s.pe if kc == 7 else None)
                tok_p = s.pe.count
                vp.on("scalar",
                      lambda h, h_=h_, bank=bank, bcol=bcol: h.activation(
                          ev_r[0:64, :], bank[0:64, :], AF.Relu,
                          bias=bcol[:, h_:h_ + 1]),
                      waits=[(s.pe, tok_p), (s.dve, s.dve.count)])
                tok_m = vp.on("scalar",
                              lambda h, h_=h_, bank=bank, nbcol=nbcol: h.activation(
                                  ev_m[0:64, :], bank[0:64, :], AF.Relu,
                                  bias=nbcol[:, h_:h_ + 1], scale=-1.0),
                              inc=s.act, sync=True)
                big_rot.set_free(bi, s.act, tok_m)
                tok_x = vp.on("scalar",
                              lambda h: h.activation(ev_e[0:64, :], ev_m[0:64, :],
                                                     AF.Exp, scale=-1.0),
                              inc=s.act)
                vp.on("vector",
                      lambda h, h_=h_, fdst=fdst: h.tensor_tensor(
                          fdst[:, h_ * 512:(h_ + 1) * 512], ev_r[0:64, :], ev_e[0:64, :],
                          op=ALU.add),
                      waits=[(s.act, tok_x)], inc=s.dve)
        tok_fqk = s.dve.count

        for tt in range(4):
            for nh in range(2):
                bank, fr, bi = big_rot.next()
                for kc in range(8):
                    vp.on("tensor",
                          lambda h, tt=tt, nh=nh, kc=kc, bank=bank: h.matmul(
                              bank[:, :],
                              xTc[:, kc * 512 + tt * 128:kc * 512 + (tt + 1) * 128],
                              wvT[:, kc * 1024 + nh * 512:kc * 1024 + (nh + 1) * 512],
                              start=(kc == 0), stop=(kc == 7)),
                          waits=[fr] if kc == 0 else (),
                          inc=s.pe if kc == 7 else None)
                tok_p = s.pe.count
                tok_ev = vp.on("vector",
                               lambda h, tt=tt, nh=nh, bank=bank: h.tensor_tensor(
                                   v_c[:, tt * 1024 + nh * 512:tt * 1024 + (nh + 1) * 512],
                                   bank[:, :], bv_bc[:, nh * 512:(nh + 1) * 512],
                                   op=ALU.add),
                               waits=[(s.pe, tok_p)], inc=s.dve)
                big_rot.set_free(bi, s.dve, tok_ev)
        tok_v = s.dve.count

        szb_wait = getattr(s, "tok_szb_prev", 0)
        first_nd = [True]
        head_state = {}

        def emit_front(h_):
            par = h_ % 2
            vz_t = vz if par == 0 else vz2
            AT_t = AT if par == 0 else AT2
            ktm_t = ktm if par == 0 else ktm2
            for tt in range(4):
                tok_vz = vp.on("vector",
                               lambda h, tt=tt, h_=h_, vz_t=vz_t: h.tensor_copy(
                                   vz_t[:, tt * 65:tt * 65 + 64],
                                   v_c[:, tt * 1024 + h_ * 64:tt * 1024 + (h_ + 1) * 64]),
                               waits=([(s.dve, tok_v), (s.pe, s.pe.count)]
                                      if tt == 0 else ()),
                               inc=s.dve if tt == 3 else None)
            for cki in range(4):
                bank, fr, bi = big_rot.next()
                tok_at = vp.on("tensor",
                               lambda h, h_=h_, cki=cki, bank=bank: h.matmul(
                                   bank[:, :],
                                   fk64[:, h_ * 512 + cki * 128:h_ * 512 + (cki + 1) * 128],
                                   fq64[:, h_ * 512:(h_ + 1) * 512],
                                   start=True, stop=True),
                               waits=[(s.dve, tok_fqk), fr], inc=s.pe)
                tok_ae = vp.on("vector",
                               lambda h, cki=cki, bank=bank, AT_t=AT_t: h.tensor_tensor(
                                   AT_t[:, cki * 512:(cki + 1) * 512], bank[:, :],
                                   masks[:, cki * 512:(cki + 1) * 512], op=ALU.mult),
                               waits=[(s.pe, tok_at)], inc=s.dve)
                big_rot.set_free(bi, s.dve, tok_ae)
            tok_AT = s.dve.count
            for tt in range(4):
                bank, fr, bi = tr_rot.next()
                bb = bank.ap().bitcast(BF16)[:, 0:64]
                tok_tr = vp.on("tensor",
                               lambda h, h_=h_, tt=tt, bb=bb: h.transpose(
                                   bb,
                                   fk64[:, h_ * 512 + tt * 128:h_ * 512 + (tt + 1) * 128],
                                   identb[0:64, 0:64]),
                               waits=[fr], inc=s.pe)
                tok_e = vp.on("vector",
                              lambda h, tt=tt, bb=bb, ktm_t=ktm_t: h.tensor_copy(
                                  ktm_t[:, tt * 64:(tt + 1) * 64], bb),
                              waits=[(s.pe, tok_tr)], inc=s.dve)
                tr_rot.set_free(bi, s.dve, tok_e)
            head_state[h_] = (vz_t, AT_t, ktm_t, tok_vz, tok_AT, s.dve.count)

        def emit_back(h_):
            vz_t, AT_t, ktm_t, tok_vz, tok_AT, tok_ktm = head_state.pop(h_)
            for tt in range(4):
                bank, fr, bi = nd_rot.next()
                vp.on("tensor",
                      lambda h, h_=h_, tt=tt, bank=bank: h.matmul(
                          bank[:, :],
                          fq64[:, h_ * 512 + tt * 128:h_ * 512 + (tt + 1) * 128],
                          Szb[:, h_ * 65:(h_ + 1) * 65],
                          start=True, stop=False),
                      waits=[(s.dve, tok_vz), fr] +
                            ([(s.dve, szb_wait)] if first_nd[0] else []))
                first_nd[0] = False
                for cki in range(tt + 1):
                    vp.on("tensor",
                          lambda h, tt=tt, cki=cki, bank=bank, vz_t=vz_t, AT_t=AT_t:
                          h.matmul(
                              bank[:, :],
                              AT_t[:, cki * 512 + tt * 128:cki * 512 + (tt + 1) * 128],
                              vz_t[:, cki * 65:(cki + 1) * 65],
                              start=False, stop=(cki == tt)),
                          waits=[(s.dve, tok_AT)] if cki == 0 else (),
                          inc=s.pe if cki == tt else None)
                tok_nd = s.pe.count
                vp.on("vector",
                      lambda h, bank=bank: h.tensor_scalar(
                          s.col0[:, 0:1], bank[:, 64:65], 1e-6, None, op0=ALU.max),
                      waits=[(s.pe, tok_nd), (s.act, s.act.count)], sync=True)
                vp.on("vector",
                      lambda h: h.reciprocal(s.col1[:, 0:1], s.col0[:, 0:1]),
                      sync=True)
                tok_o = vp.on("vector",
                              lambda h, h_=h_, tt=tt, bank=bank: h.tensor_scalar(
                                  att[:, tt * 1024 + h_ * 64:tt * 1024 + (h_ + 1) * 64],
                                  bank[:, 0:64], s.col1[:, 0:1], None, op0=ALU.mult),
                              inc=s.dve)
                nd_rot.set_free(bi, s.dve, tok_o)
            for tt in range(4):
                vp.on("tensor",
                      lambda h, tt=tt, ktm_t=ktm_t, vz_t=vz_t: h.matmul(
                          ps_st[:, :], ktm_t[:, tt * 64:(tt + 1) * 64],
                          vz_t[:, tt * 65:(tt + 1) * 65],
                          start=(tt == 0), stop=(tt == 3)),
                      waits=([(s.dve, tok_ktm), (s.dve, getattr(s, "tok_stfree", 0))]
                             if tt == 0 else ()),
                      inc=s.pe if tt == 3 else None)
            tok_stm = s.pe.count
            vp.on("vector",
                  lambda h, h_=h_: h.tensor_tensor(
                      Sz[:, h_ * 65:(h_ + 1) * 65], ps_st[:, :],
                      Sz[:, h_ * 65:(h_ + 1) * 65], op=ALU.add),
                  waits=[(s.pe, tok_stm)], sync=True)
            s.tok_stfree = vp.on("vector",
                                 lambda h, h_=h_: h.tensor_copy(
                                     Szb[:, h_ * 65:(h_ + 1) * 65],
                                     Sz[:, h_ * 65:(h_ + 1) * 65]),
                                 inc=s.dve)

        prev_h = None
        for h_ in range(H):
            emit_front(h_)
            if prev_h is not None:
                emit_back(prev_h)
            prev_h = h_
        emit_back(prev_h)
        s.tok_szb_prev = s.dve.count

        for tt in range(4):
            for j in range(8):
                bank, fr, bi = tr_rot.next()
                bb = bank.ap().bitcast(BF16)[:, 0:128]
                tok_tr = vp.on("tensor",
                               lambda h, tt=tt, j=j, bb=bb: h.transpose(
                                   bb,
                                   att[:, tt * D + j * 128:tt * D + (j + 1) * 128],
                                   identb[:, :]),
                               waits=[(s.dve, s.dve.count), fr],
                               inc=s.pe)
                tok_e = vp.on("vector",
                              lambda h, tt=tt, j=j, bb=bb: h.tensor_copy(
                                  attT[:, j * 512 + tt * 128:j * 512 + (tt + 1) * 128],
                                  bb),
                              waits=[(s.pe, tok_tr)], inc=s.dve)
                tr_rot.set_free(bi, s.dve, tok_e)
        tok_attT = s.dve.count

        for tt in range(4):
            for nh in range(2):
                bank, fr, bi = big_rot.next()
                for kc in range(8):
                    vp.on("tensor",
                          lambda h, tt=tt, nh=nh, kc=kc, bank=bank: h.matmul(
                              bank[:, :],
                              attT[:, kc * 512 + tt * 128:kc * 512 + (tt + 1) * 128],
                              woT[:, kc * 1024 + nh * 512:kc * 1024 + (nh + 1) * 512],
                              start=(kc == 0), stop=(kc == 7)),
                          waits=([(s.dve, tok_attT), fr] if kc == 0 else ()),
                          inc=s.pe if kc == 7 else None)
                tok_p = s.pe.count
                vp.on("vector",
                      lambda h, nh=nh, bank=bank: h.tensor_tensor(
                          sqb[:, 0:512], bank[:, :], bo_bc[:, nh * 512:(nh + 1) * 512],
                          op=ALU.add),
                      waits=[(s.pe, tok_p)], sync=True)
                tok_res = vp.on("vector",
                                lambda h, tt=tt, nh=nh, xc_c=xc_c: h.tensor_tensor(
                                    xc_c[:, tt * D + nh * 512:tt * D + (nh + 1) * 512],
                                    sqb[:, 0:512],
                                    xc_c[:, tt * D + nh * 512:tt * D + (nh + 1) * 512],
                                    op=ALU.add),
                                inc=s.dve)
                big_rot.set_free(bi, s.dve, tok_res)
        for tt in range(4):
            tok_ln = layernorm_inplace(vp, s, xc_c[:, tt * D:(tt + 1) * D], sqb)
        tok_st1 = vp.on("gpsimd",
                        lambda h, cc=cc, xc_c=xc_c: h.dma_start(
                            out=s1r[cc],
                            in_=xc_c.ap().rearrange("p (tt d) -> p tt d", tt=4)),
                        waits=[(s.dve, tok_ln)], inc=s.dst, amt=16)
        xc_store_tok[cc % 2] = tok_st1

    vp.wait("gpsimd", s.dst, s.dst.count)
    return stk




def build_phase_c(nc, vp, io):
    s = mk_sems(vp, "c")
    H, DH = 8, 128
    SCALE = 1.0 / math.sqrt(DH)

    attnT_hbm = nc.dram_tensor("c_attnT", [1024, T], BF16, kind="Internal").ap()

    stk = ExitStack()
    sb = lambda name, shape, dt: stk.enter_context(nc.sbuf_tensor(name, shape, dt))
    psb = lambda name, shape, dt=F32: stk.enter_context(nc.psum_tensor(name, shape, dt))

    xT = sb("c_xT", [128, 8 * T], BF16)
    identb = sb("c_identb", [128, 128], BF16)
    amask = sb("c_amask", [128, 128], F32)
    hs_sb = sb("c_hs", [1, 8], F32)
    dirs_b = sb("c_dirs", [3, 8], BF16)
    dirs_f = sb("c_dirsf", [3, 8], F32)
    s.col0 = sb("c_col0", [128, 1], F32)
    s.col1 = sb("c_col1", [128, 1], F32)
    s.col2 = sb("c_col2", [128, 1], F32)

    ps_tr1 = psb("c_pstr1", [128, 512], BF16)
    ps_tr2 = psb("c_pstr2", [128, 512], BF16)
    ps_b1 = psb("c_psb1", [128, 512])
    ps_b2 = psb("c_psb2", [128, 512])
    ps_b3 = psb("c_psb3", [128, 512])
    ps_av = psb("c_psav", [128, 512])
    tr_rot = PsRot([ps_tr1, ps_tr2])
    big_rot = PsRot([ps_b1, ps_b2, ps_b3])

    make_identity(vp, identb)
    vp.on("gpsimd", lambda h: h.memset(amask[:, :], 0.0))
    vp.on("gpsimd",
          lambda h: h.affine_select(
              out=amask[:, :], in_=amask[:, :], compare_op=ALU.is_ge, fill=-60000.0,
              base=0, pattern=[[-1, 128]], channel_multiplier=1),
          inc=s.pool)
    vp.wait("tensor", s.pool, 1)

    tok = vp.on("sync", lambda h: h.dma_start(
        out=hs_sb[:, :], in_=io["head_scales"].rearrange("(a e) -> a e", a=1)),
        inc=s.dld, amt=16)
    for h_ in range(8):
        tok = vp.on("sync",
                    lambda hh, h_=h_: hh.dma_start(
                        out=dirs_f[:, h_:h_ + 1],
                        in_=io["head_dirs"][h_].rearrange("(a b) -> a b", b=1)),
                    inc=s.dld, amt=16)
    vp.on("vector", lambda h: h.tensor_copy(dirs_b[:, :], dirs_f[:, :]),
          waits=[(s.dld, s.dld.count)], inc=s.dve)

    s2r = io["s2"].rearrange("(tb tt p) d -> tb p tt d", p=128, tt=4)
    s3r = io["s3"].rearrange("(tb tt p) d -> tb p tt d", p=128, tt=4)
    with nc.sbuf_tensor("c_sc0", [128, 4 * D], F32) as sc0, \
         nc.sbuf_tensor("c_scb0", [128, 4 * D], BF16) as scb0:
        for tb in range(8):
            tok_ld = vp.on("sync",
                           lambda h, tb=tb: h.dma_start(
                               out=sc0.ap().rearrange("p (tt d) -> p tt d", tt=4),
                               in_=s2r[tb]),
                           waits=[(s.dve, s.dve.count)],
                           inc=s.dld, amt=16)
            tok_cv = vp.on("vector", lambda h: h.tensor_copy(scb0[:, :], sc0[:, :]),
                           waits=[(s.dld, tok_ld), (s.pe, s.pe.count)], inc=s.dve)
            for tt in range(4):
                for j in range(8):
                    bank, fr, bi = tr_rot.next()
                    tok_tr = vp.on("tensor",
                                   lambda h, tt=tt, j=j, bank=bank: h.transpose(
                                       bank[:, 0:128],
                                       scb0[:, tt * D + j * 128:tt * D + (j + 1) * 128],
                                       identb[:, :]),
                                   waits=[(s.dve, tok_cv), fr], inc=s.pe)
                    tok_e = vp.on("vector",
                                  lambda h, tb=tb, tt=tt, j=j, bank=bank: h.tensor_copy(
                                      xT[:, j * T + tb * 512 + tt * 128:
                                         j * T + tb * 512 + (tt + 1) * 128],
                                      bank[:, 0:128]),
                                  waits=[(s.pe, tok_tr)], inc=s.dve)
                    tr_rot.set_free(bi, s.dve, tok_e)
        vp.barrier(s.bar)

    with nc.sbuf_tensor("c_whs", [128, 8 * 384], F32) as whs, \
         nc.sbuf_tensor("c_whb", [128, 8 * 384], BF16) as whb, \
         nc.sbuf_tensor("c_whb2", [128, 8 * 384], BF16) as whb2, \
         nc.sbuf_tensor("c_QT", [128, T], BF16) as QT, \
         nc.sbuf_tensor("c_KT", [128, T], BF16) as KT, \
         nc.sbuf_tensor("c_Vtm", [128, 32 * 128], BF16) as Vtm, \
         nc.sbuf_tensor("c_dcols", [128, 8], F32) as dcols, \
         nc.sbuf_tensor("c_A4a", [128, 4 * T], BF16) as A4a, \
         nc.sbuf_tensor("c_A4b", [128, 4 * T], BF16) as A4b, \
         nc.sbuf_tensor("c_ATg1", [128, 512], BF16) as ATg1, \
         nc.sbuf_tensor("c_ATg2", [128, 512], BF16) as ATg2, \
         nc.sbuf_tensor("c_avb", [128, 512], BF16) as avb, \
         nc.sbuf_tensor("c_avb2", [128, 512], BF16) as avb2, \
         nc.sbuf_tensor("c_qp", [1, T], BF16) as qp_row, \
         nc.sbuf_tensor("c_kpa", [1, T], BF16) as kpa_row:
        wh_pe_done = [0, 0]
        wh_cv_tok = [0, 0]
        s.a4_free = [0, 0]
        s.avb_store = [0, 0]

        wh_cv_any = [0]

        def emit_wh_load(h_):
            par = h_ % 2
            whs_t = whs
            whb_t = whb if par == 0 else whb2
            tok_wld = 0
            for kc in range(8):
                for jj, off in enumerate((0, 1024, 2048)):
                    tok_wld = vp.on("sync",
                                    lambda h, kc=kc, jj=jj, off=off, h_=h_,
                                    whs_t=whs_t: h.dma_start(
                                        out=whs_t[:, kc * 384 + jj * 128:
                                                  kc * 384 + (jj + 1) * 128],
                                        in_=io["Wqkv"][kc * 128:(kc + 1) * 128,
                                                       off + h_ * 128:off + (h_ + 1) * 128]),
                                    waits=([(s.dve, wh_cv_any[0])]
                                           if (kc == 0 and jj == 0) else ()),
                                    inc=s.dld, amt=16)
            wh_cv_tok[par] = vp.on("vector",
                                   lambda h, whs_t=whs_t, whb_t=whb_t: h.tensor_copy(
                                       whb_t[:, :], whs_t[:, :]),
                                   waits=[(s.dld, tok_wld), (s.pe, wh_pe_done[par])],
                                   inc=s.dve)
            wh_cv_any[0] = wh_cv_tok[par]

        emit_wh_load(0)
        for h_ in range(H):
            whb_t = whb if h_ % 2 == 0 else whb2
            tok_wcv = wh_cv_tok[h_ % 2]
            if h_ + 1 < H:
                emit_wh_load(h_ + 1)

            for dst, jj, scale in ((QT, 0, SCALE), (KT, 1, None)):
                for g in range(8):
                    bank, fr, bi = big_rot.next()
                    for kc in range(8):
                        vp.on("tensor",
                              lambda h, dst=dst, jj=jj, g=g, kc=kc, bank=bank,
                              whb_t=whb_t: h.matmul(
                                  bank[:, :],
                                  whb_t[:, kc * 384 + jj * 128:kc * 384 + (jj + 1) * 128],
                                  xT[:, kc * T + g * 512:kc * T + (g + 1) * 512],
                                  start=(kc == 0), stop=(kc == 7)),
                              waits=[(s.dve, tok_wcv), fr] if kc == 0 else (),
                              inc=s.pe if kc == 7 else None)
                    tok_p = s.pe.count
                    if scale is None:
                        tok_e = vp.on("vector",
                                      lambda h, dst=dst, g=g, bank=bank: h.tensor_copy(
                                          dst[:, g * 512:(g + 1) * 512], bank[:, :]),
                                      waits=[(s.pe, tok_p)], inc=s.dve)
                    else:
                        tok_e = vp.on("vector",
                                      lambda h, dst=dst, g=g, bank=bank, scale=scale:
                                      h.tensor_scalar(dst[:, g * 512:(g + 1) * 512],
                                                      bank[:, :], scale, None,
                                                      op0=ALU.mult),
                                      waits=[(s.pe, tok_p)], inc=s.dve)
                    big_rot.set_free(bi, s.dve, tok_e)
            for g in range(8):
                bank, fr, bi = big_rot.next()
                for kc in range(8):
                    vp.on("tensor",
                          lambda h, g=g, kc=kc, bank=bank, whb_t=whb_t: h.matmul(
                              bank[:, :],
                              whb_t[:, kc * 384 + 2 * 128:kc * 384 + 3 * 128],
                              xT[:, kc * T + g * 512:kc * T + (g + 1) * 512],
                              start=(kc == 0), stop=(kc == 7)),
                          waits=[fr] if kc == 0 else (),
                          inc=s.pe if kc == 7 else None)
                tok_p = s.pe.count
                tok_e = vp.on("vector",
                              lambda h, g=g, bank=bank: h.tensor_copy(
                                  A4a[:, g * 512:(g + 1) * 512], bank[:, :]),
                              waits=[(s.pe, tok_p)], inc=s.dve)
                big_rot.set_free(bi, s.dve, tok_e)
            tok_vt = s.dve.count
            for tt in range(32):
                bank, fr, bi = tr_rot.next()
                tok_tr = vp.on("tensor",
                               lambda h, tt=tt, bank=bank: h.transpose(
                                   bank[:, 0:128], A4a[:, tt * 128:(tt + 1) * 128],
                                   identb[:, :]),
                               waits=[(s.dve, tok_vt), fr], inc=s.pe)
                tok_e = vp.on("vector",
                              lambda h, tt=tt, bank=bank: h.tensor_copy(
                                  Vtm[:, tt * 128:(tt + 1) * 128], bank[:, 0:128]),
                              waits=[(s.pe, tok_tr)], inc=s.dve)
                tr_rot.set_free(bi, s.dve, tok_e)

            wh_pe_done[h_ % 2] = s.pe.count
            for g in range(8):
                bank, fr, bi = big_rot.next()
                tk = vp.on("tensor",
                           lambda h, g=g, h_=h_, bank=bank: h.matmul(
                               bank[0:1, :], dirs_b[:, h_:h_ + 1],
                               QT[0:3, g * 512:(g + 1) * 512], start=True, stop=True),
                           waits=[(s.dve, s.dve.count), fr], inc=s.pe)
                te = vp.on("vector",
                           lambda h, g=g, bank=bank: h.tensor_copy(
                               qp_row[:, g * 512:(g + 1) * 512], bank[0:1, :]),
                           waits=[(s.pe, tk)], inc=s.dve)
                big_rot.set_free(bi, s.dve, te)
            for g in range(8):
                bank, fr, bi = big_rot.next()
                tk = vp.on("tensor",
                           lambda h, g=g, h_=h_, bank=bank: h.matmul(
                               bank[0:1, :], dirs_b[:, h_:h_ + 1],
                               KT[0:3, g * 512:(g + 1) * 512], start=True, stop=True),
                           waits=[fr], inc=s.pe)
                te = vp.on("vector",
                           lambda h, g=g, h_=h_, bank=bank: h.tensor_scalar(
                               kpa_row[:, g * 512:(g + 1) * 512], bank[0:1, :],
                               hs_sb[0:1, h_:h_ + 1], 1.0 / SCALE,
                               op0=ALU.mult, op1=ALU.mult),
                           waits=[(s.pe, tk)], inc=s.dve)
                big_rot.set_free(bi, s.dve, te)
            tok_rows = s.dve.count

            tokA = {}

            def emit_S(qg):
                A4 = A4a if qg % 2 == 0 else A4b
                Wg = (qg + 1) * 512
                for li in range(4):
                    i = qg * 4 + li
                    W = (i + 1) * 128
                    ng = (W + 511) // 512
                    for g in range(ng):
                        w = min(512, W - g * 512)
                        bank, fr, bi = big_rot.next()
                        vp.on("tensor",
                              lambda h, i=i, g=g, w=w, bank=bank: h.matmul(
                                  bank[:, 0:w], QT[:, i * 128:(i + 1) * 128],
                                  KT[:, g * 512:g * 512 + w],
                                  start=True, stop=False),
                              waits=[(s.dve, tok_rows), fr])
                        tok_sm = vp.on("tensor",
                                       lambda h, i=i, g=g, w=w, bank=bank: h.matmul(
                                           bank[:, 0:w],
                                           qp_row[:, i * 128:(i + 1) * 128],
                                           kpa_row[:, g * 512:g * 512 + w],
                                           start=False, stop=True),
                                       inc=s.pe)
                        exp_waits = [(s.pe, tok_sm)]
                        if g == 0:
                            exp_waits.append((s.dve, s.dve.count))
                            exp_waits.append((s.pe, s.a4_free[qg % 2]))
                        if g == ng - 1:
                            tok_msk = vp.on("vector",
                                            lambda h, w=w, bank=bank: h.tensor_tensor(
                                                bank[:, w - 128:w], bank[:, w - 128:w],
                                                amask[:, :], op=ALU.add),
                                            waits=[(s.pe, tok_sm)], inc=s.dve)
                            exp_waits.append((s.dve, tok_msk))
                        tok_exp = vp.on("scalar",
                                        lambda h, li=li, g=g, w=w, bank=bank, A4=A4:
                                        h.activation(
                                            A4[:, li * T + g * 512:li * T + g * 512 + w],
                                            bank[:, 0:w], AF.Exp,
                                            accum_out=dcols[:, g:g + 1]),
                                        waits=exp_waits, inc=s.act)
                        big_rot.set_free(bi, s.act, tok_exp)
                    vp.on("vector",
                          lambda h, ng=ng: h.reduce_sum(s.col1[:, 0:1],
                                                        dcols[:, 0:ng], axis=AX.X),
                          waits=[(s.act, s.act.count)], sync=True)
                    vp.on("vector", lambda h: h.reciprocal(s.col2[:, 0:1],
                                                           s.col1[:, 0:1]),
                          sync=True)
                    tok_nr = vp.on("vector",
                                   lambda h, li=li, W=W, A4=A4: h.tensor_scalar(
                                       A4[:, li * T:li * T + W], A4[:, li * T:li * T + W],
                                       s.col2[:, 0:1], None, op0=ALU.mult),
                                   inc=s.dve)
                    if W < Wg:
                        vp.on("gpsimd",
                              lambda h, li=li, W=W, Wg=Wg, A4=A4: h.memset(
                                  A4[:, li * T + W:li * T + Wg], 0.0),
                              waits=[(s.dve, tok_nr), (s.pe, s.pe.count)], inc=s.pool)
                tokA[qg] = (s.dve.count, s.pool.count)

            def emit_AV(qg, h_):
                A4 = A4a if qg % 2 == 0 else A4b
                tok_A = tokA[qg]
                atg = [ATg1, ATg2]
                atg_free = [None, None]
                nkt = (qg + 1) * 4
                pend = None

                def emit_av_mm(kt, ab, tok_ae):
                    tok_av = vp.on("tensor",
                                   lambda h, kt=kt, ab=ab: h.matmul(
                                       ps_av[:, :], Vtm[:, kt * 128:(kt + 1) * 128],
                                       ab[:, :], start=(kt == 0),
                                       stop=(kt == nkt - 1)),
                                   waits=[(s.dve, tok_ae)] +
                                         ([(s.dve, getattr(s, "tok_avfree", 0))]
                                          if kt == 0 else []),
                                   inc=s.pe)
                    atg_free[kt % 2] = (s.pe, tok_av)

                for kt in range(nkt):
                    bank, fr, bi = tr_rot.next()
                    for li in range(4):
                        vp.on("tensor",
                              lambda h, li=li, kt=kt, bank=bank, A4=A4: h.transpose(
                                  bank[:, li * 128:(li + 1) * 128],
                                  A4[:, li * T + kt * 128:li * T + (kt + 1) * 128],
                                  identb[:, :]),
                              waits=([(s.dve, tok_A[0]), (s.pool, tok_A[1]), fr]
                                     if li == 0 else ()),
                              inc=s.pe if li == 3 else None)
                    tok_tr = s.pe.count
                    if pend is not None:
                        emit_av_mm(*pend)
                    ab = atg[kt % 2]
                    tok_ae = vp.on("vector",
                                   lambda h, ab=ab, bank=bank: h.tensor_copy(
                                       ab[:, :], bank[:, :]),
                                   waits=[(s.pe, tok_tr)] +
                                         ([atg_free[kt % 2]] if atg_free[kt % 2] else []),
                                   inc=s.dve)
                    tr_rot.set_free(bi, s.dve, tok_ae)
                    pend = (kt, ab, tok_ae)
                emit_av_mm(*pend)
                avb_t = avb if qg % 2 == 0 else avb2
                tok_ave = vp.on("vector",
                                lambda h, avb_t=avb_t: h.tensor_copy(avb_t[:, :],
                                                                     ps_av[:, :]),
                                waits=[(s.pe, s.pe.count),
                                       (s.dst, s.avb_store[qg % 2])],
                                inc=s.dve)
                s.tok_avfree = tok_ave
                s.avb_store[qg % 2] = vp.on(
                    "gpsimd",
                    lambda h, h_=h_, qg=qg, avb_t=avb_t: h.dma_start(
                        out=attnT_hbm[h_ * 128:(h_ + 1) * 128,
                                      qg * 512:(qg + 1) * 512],
                        in_=avb_t[:, :]),
                    waits=[(s.dve, tok_ave)], inc=s.dst, amt=16)
                s.a4_free[qg % 2] = s.pe.count

            prev = None
            for qg in range(8):
                emit_S(qg)
                if prev is not None:
                    emit_AV(prev, h_)
                prev = qg
            emit_AV(prev, h_)

        vp.barrier(s.bar)

    with nc.sbuf_tensor("c_wout", [128, 8 * 1024], BF16) as wout, \
         nc.sbuf_tensor("c_wstage", [128, 1024], F32) as wstage, \
         nc.sbuf_tensor("c_aT", [128, 8 * 512], BF16) as aT, \
         nc.sbuf_tensor("c_aT2", [128, 8 * 512], BF16) as aT2, \
         nc.sbuf_tensor("c_sc", [128, 4 * D], F32) as sc, \
         nc.sbuf_tensor("c_sc2", [128, 4 * D], F32) as sc2, \
         nc.sbuf_tensor("c_sqb", [128, D], F32) as sqb:
        for kc in range(8):
            tok_ld = vp.on("sync",
                           lambda h, kc=kc: h.dma_start(
                               out=wstage[:, :],
                               in_=io["Wout"][kc * 128:(kc + 1) * 128, :]),
                           waits=[(s.dve, s.dve.count)],
                           inc=s.dld, amt=16)
            vp.on("vector",
                  lambda h, kc=kc: h.tensor_copy(wout[:, kc * 1024:(kc + 1) * 1024],
                                                 wstage[:, :]),
                  waits=[(s.dld, tok_ld)], inc=s.dve, sync=True)
        aTr = attnT_hbm.rearrange("(kc p) t -> p kc t", p=128)
        ld_toks = [(0, 0), (0, 0)]
        st_toks = [0, 0]
        pe_done = [0, 0]

        def emit_loads(tb):
            par = tb % 2
            aT_t = aT if par == 0 else aT2
            sc_t = sc if par == 0 else sc2
            ta = vp.on("sync",
                       lambda h, tb=tb, aT_t=aT_t: h.dma_start(
                           out=aT_t.ap().rearrange("p (kc t) -> p kc t", kc=8),
                           in_=aTr[:, :, tb * 512:(tb + 1) * 512]),
                       waits=[(s.pe, pe_done[par])],
                       inc=s.dld, amt=16)
            ts_ = vp.on("sync",
                        lambda h, tb=tb, sc_t=sc_t: h.dma_start(
                            out=sc_t.ap().rearrange("p (tt d) -> p tt d", tt=4),
                            in_=s2r[tb]),
                        waits=[(s.dst, st_toks[par])],
                        inc=s.dld, amt=16)
            ld_toks[par] = (ta, ts_)

        emit_loads(0)
        for tb in range(8):
            par = tb % 2
            aT_t = aT if par == 0 else aT2
            sc_t = sc if par == 0 else sc2
            tok_lda, tok_lds = ld_toks[par]
            if tb + 1 < 8:
                emit_loads(tb + 1)
            for tt in range(4):
                for nh in range(2):
                    bank, fr, bi = big_rot.next()
                    for kc in range(8):
                        vp.on("tensor",
                              lambda h, tt=tt, nh=nh, kc=kc, bank=bank, aT_t=aT_t:
                              h.matmul(
                                  bank[:, :],
                                  aT_t[:, kc * 512 + tt * 128:kc * 512 + (tt + 1) * 128],
                                  wout[:, kc * 1024 + nh * 512:kc * 1024 + (nh + 1) * 512],
                                  start=(kc == 0), stop=(kc == 7)),
                              waits=[(s.dld, tok_lda), fr] if kc == 0 else (),
                              inc=s.pe if kc == 7 else None)
                    tok_p = s.pe.count
                    tok_res = vp.on("vector",
                                    lambda h, tt=tt, nh=nh, bank=bank, sc_t=sc_t:
                                    h.tensor_tensor(
                                        sc_t[:, tt * D + nh * 512:tt * D + (nh + 1) * 512],
                                        bank[:, :],
                                        sc_t[:, tt * D + nh * 512:tt * D + (nh + 1) * 512],
                                        op=ALU.add),
                                    waits=[(s.pe, tok_p), (s.dld, tok_lds)],
                                    inc=s.dve)
                    big_rot.set_free(bi, s.dve, tok_res)
            pe_done[par] = s.pe.count
            for tt in range(4):
                tok_ln = layernorm_inplace(vp, s, sc_t[:, tt * D:(tt + 1) * D], sqb)
            st_toks[par] = vp.on("gpsimd",
                                 lambda h, tb=tb, sc_t=sc_t: h.dma_start(
                                     out=s3r[tb],
                                     in_=sc_t.ap().rearrange("p (tt d) -> p tt d",
                                                             tt=4)),
                                 waits=[(s.dve, tok_ln)], inc=s.dst, amt=16)

    vp.wait("gpsimd", s.dst, s.dst.count)
    return stk




def load_convert_bg(vp, s, w_hbm, rows, cols, wstages, dst, pe_gate):
    n_rt = rows // 128
    lag = 0 if wstages[0] is wstages[1] else 1

    def chunk(rt):
        ws = wstages[rt % 2]
        tok_ld = vp.on("sync",
                       lambda h, rt=rt, ws=ws: h.dma_start(
                           out=ws[:, 0:cols],
                           in_=w_hbm[rt * 128:(rt + 1) * 128, :]),
                       waits=[(s.pool, max(0, s.pool.count - lag))],
                       inc=s.dld, amt=16)
        vp.on("gpsimd",
              lambda h, rt=rt, ws=ws: h.tensor_copy(
                  dst[:, rt * cols:(rt + 1) * cols], ws[:, 0:cols]),
              waits=[(s.dld, tok_ld), (s.pe, pe_gate())], inc=s.pool)

    return [lambda rt=rt: chunk(rt) for rt in range(n_rt)]


def load_convert(vp, s, w_hbm, rows, cols, wstage, dst, pe_gate=None):
    n_rt = rows // 128
    for rt in range(n_rt):
        tok_ld = vp.on("sync",
                       lambda h, rt=rt: h.dma_start(
                           out=wstage[:, 0:cols],
                           in_=w_hbm[rt * 128:(rt + 1) * 128, :]),
                       waits=[(s.dve, s.dve.count)],
                       inc=s.dld, amt=16)
        vp.on("vector",
              lambda h, rt=rt: h.tensor_copy(dst[:, rt * cols:(rt + 1) * cols],
                                             wstage[:, 0:cols]),
              waits=[(s.dld, tok_ld),
                     (s.pe, s.pe.count if pe_gate is None else pe_gate)],
              inc=s.dve)


def build_phase_d(nc, vp, io):
    s = mk_sems(vp, "d")
    E = 16

    acc = nc.dram_tensor("d_acc", [T, D], F32, kind="Internal").ap()

    stk = ExitStack()
    sb = lambda name, shape, dt: stk.enter_context(nc.sbuf_tensor(name, shape, dt))
    psb = lambda name, shape, dt=F32: stk.enter_context(nc.psum_tensor(name, shape, dt))

    xT = sb("d_xT", [128, 8 * T], BF16)
    comb = sb("d_comb", [128, 32 * 16], F32)
    ones_row = sb("d_ones", [1, 128], F32)
    s.col0 = sb("d_col0", [128, 1], F32)
    s.col1 = sb("d_col1", [128, 1], F32)
    s.col2 = sb("d_col2", [128, 1], F32)
    rstk = ExitStack()
    rsb = lambda name, shape, dt: rstk.enter_context(nc.sbuf_tensor(name, shape, dt))
    ident32 = rsb("d_ident32", [128, 128], F32)
    rwT32 = rsb("d_rwT", [128, 8 * 16], F32)
    bias_bc = rsb("d_biasbc", [128, 16], F32)
    scores = rsb("d_scores", [128, 32 * 16], F32)
    r_sb = rsb("d_rsb", [128, 32 * 16], F32)
    r_eq = rsb("d_req", [128, 32 * 16], F32)
    r_msk = rsb("d_rmsk", [128, 32 * 16], F32)
    r_wrk = rsb("d_rwrk", [128, 32 * 16], F32)
    r_tw = rsb("d_rtw", [128, 32 * 16], F32)
    gm1 = rsb("d_gm1", [128, 32 * 4], F32)
    gm2 = rsb("d_gm2", [128, 32 * 4], F32)
    c32a = rsb("d_c32a", [128, 32], F32)
    c32b = rsb("d_c32b", [128, 32], F32)

    ps_tr1 = psb("d_pstr1", [128, 128])
    ps_tr2 = psb("d_pstr2", [128, 128])
    ps_r = psb("d_psr", [128, 16])
    ps_a = psb("d_psa", [128, 512])
    ps_b = psb("d_psb", [128, 512])
    ps_c = psb("d_psc", [128, 512])
    ps_d = psb("d_psd", [128, 512])
    tr_rot = PsRot([ps_tr1, ps_tr2])
    hu_rot = PsRot([ps_a, ps_b])
    y_rot = PsRot([ps_c, ps_d])

    make_identity(vp, ident32)
    vp.on("gpsimd", lambda h: h.memset(ones_row[:, :], 1.0), inc=s.pool)
    vp.wait("tensor", s.pool, 1)

    s3r = io["s3"].rearrange("(tb tt p) d -> tb p tt d", p=128, tt=4)
    outr = io["out"].rearrange("(tb tt p) d -> tb p tt d", p=128, tt=4)
    accr = acc.rearrange("(g tt p) d -> g p tt d", p=128, tt=4)

    with nc.sbuf_tensor("d_rstage", [128, 1024], F32) as rstage:
        tok = vp.on("sync", lambda h: h.dma_start(out=rstage[0:16, :],
                                                  in_=io["router_w"]),
                    inc=s.dld, amt=16)
        vp.wait("tensor", s.dld, tok)
        for j in range(8):
            bank, fr, bi = tr_rot.next()
            tk = vp.on("tensor",
                       lambda h, j=j, bank=bank: h.transpose(
                           bank[:, 0:16], rstage[0:16, j * 128:(j + 1) * 128],
                           ident32[0:16, 0:16]),
                       waits=[fr], inc=s.pe)
            te = vp.on("vector",
                       lambda h, j=j, bank=bank: h.tensor_copy(
                           rwT32[:, j * 16:(j + 1) * 16], bank[:, 0:16]),
                       waits=[(s.pe, tk)], inc=s.dve)
            tr_rot.set_free(bi, s.dve, te)
        tokb = vp.on("sync", lambda h: h.dma_start(
            out=rstage[0:1, 0:16], in_=io["bias"].rearrange("(a e) -> a e", a=1)),
            waits=[(s.pe, s.pe.count)], inc=s.dld, amt=16)
        tk = vp.on("tensor", lambda h: h.matmul(ps_r[:, :], ones_row[0:1, :],
                                                rstage[0:1, 0:16], start=True,
                                                stop=True),
                   waits=[(s.dld, tokb)], inc=s.pe)
        vp.on("vector", lambda h: h.tensor_copy(bias_bc[:, :], ps_r[:, :]),
              waits=[(s.pe, tk)], inc=s.dve)
        vp.barrier(s.bar)

    with nc.sbuf_tensor("d_sc0", [128, 4 * D], F32) as sc0, \
         nc.sbuf_tensor("d_xT32", [128, 8 * 128], F32) as xT32:
        for tb in range(8):
            tok_ld = vp.on("sync",
                           lambda h, tb=tb: h.dma_start(
                               out=sc0.ap().rearrange("p (tt d) -> p tt d", tt=4),
                               in_=s3r[tb]),
                           waits=[(s.pe, s.pe.count)],
                           inc=s.dld, amt=16)
            for tt in range(4):
                for j in range(8):
                    bank, fr, bi = tr_rot.next()
                    tok_tr = vp.on("tensor",
                                   lambda h, tt=tt, j=j, bank=bank: h.transpose(
                                       bank[:, :],
                                       sc0[:, tt * D + j * 128:tt * D + (j + 1) * 128],
                                       ident32[:, :]),
                                   waits=[(s.dld, tok_ld), fr], inc=s.pe)
                    vp.on("vector",
                          lambda h, tb=tb, tt=tt, j=j, bank=bank: h.tensor_copy(
                              xT[:, j * T + tb * 512 + tt * 128:
                                 j * T + tb * 512 + (tt + 1) * 128],
                              bank[:, :]),
                          waits=[(s.pe, tok_tr)])
                    tok_e = vp.on("vector",
                                  lambda h, j=j, bank=bank: h.tensor_copy(
                                      xT32[:, j * 128:(j + 1) * 128], bank[:, :]),
                                  inc=s.dve)
                    tr_rot.set_free(bi, s.dve, tok_e)
                tok_x32 = s.dve.count
                for j in range(8):
                    vp.on("tensor",
                          lambda h, j=j: h.matmul(
                              ps_r[:, :], xT32[:, j * 128:(j + 1) * 128],
                              rwT32[:, j * 16:(j + 1) * 16],
                              start=(j == 0), stop=(j == 7)),
                          waits=([(s.dve, tok_x32), (s.act, s.act.count)]
                                 if j == 0 else ()),
                          inc=s.pe if j == 7 else None)
                vp.on("scalar",
                      lambda h, tb=tb, tt=tt: h.activation(
                          scores[:, (tb * 4 + tt) * 16:(tb * 4 + tt + 1) * 16],
                          ps_r[:, :], AF.Sigmoid),
                      waits=[(s.pe, s.pe.count)], inc=s.act)
        vp.barrier(s.bar)

    NT, NGv, EPGv = 32, 4, 4
    v3 = lambda buf: buf.ap().rearrange("p (o e) -> p o e", e=16)
    v4 = lambda buf: buf.ap().rearrange("p (o g j) -> p o g j", g=4, j=4)
    vg = lambda buf: buf.ap().rearrange("p (o g) -> p o g", g=4)
    B = [128, NT, 16]
    B4 = [128, NT, 4, 4]
    BG = [128, NT, 4]
    ro = lambda fn: vp.on("vector", fn, sync=True)
    ro(lambda h: h.tensor_tensor(
        v3(r_sb), v3(scores),
        bias_bc.ap().rearrange("p (o e) -> p o e", o=1).broadcast_to(B),
        op=ALU.add))
    ro(lambda h: h.reduce_max(vg(gm1), v4(r_sb), axis=AX.X))
    ro(lambda h: h.tensor_tensor(
        v4(r_eq), v4(r_sb),
        gm1.ap().rearrange("p (o g j) -> p o g j", g=4, j=1).broadcast_to(B4),
        op=ALU.is_ge))
    ro(lambda h: h.scalar_tensor_tensor(v4(r_wrk), v4(r_eq), -1e9, v4(r_sb),
                                        op0=ALU.mult, op1=ALU.add))
    ro(lambda h: h.reduce_max(vg(gm2), v4(r_wrk), axis=AX.X))
    ro(lambda h: h.tensor_tensor(vg(gm1), vg(gm1), vg(gm2), op=ALU.add))
    ro(lambda h: h.reduce_max(c32a.ap(), vg(gm1), axis=AX.X))
    ro(lambda h: h.tensor_tensor(
        vg(gm2), vg(gm1),
        c32a.ap().rearrange("p (o g) -> p o g", g=1).broadcast_to(BG),
        op=ALU.is_ge))
    ro(lambda h: h.scalar_tensor_tensor(vg(gm2), vg(gm2), -1e9, vg(gm1),
                                        op0=ALU.mult, op1=ALU.add))
    ro(lambda h: h.reduce_max(c32b.ap(), vg(gm2), axis=AX.X))
    ro(lambda h: h.tensor_tensor(
        vg(gm2), vg(gm1),
        c32b.ap().rearrange("p (o g) -> p o g", g=1).broadcast_to(BG),
        op=ALU.is_ge))
    ro(lambda h: h.tensor_scalar(r_wrk.ap(), r_sb.ap(), 1.0, None, op0=ALU.add))
    ro(lambda h: h.tensor_tensor(
        v4(r_msk), v4(r_wrk),
        gm2.ap().rearrange("p (o g j) -> p o g j", g=4, j=1).broadcast_to(B4),
        op=ALU.mult))
    ro(lambda h: h.tensor_scalar(r_msk.ap(), r_msk.ap(), -1.0, None, op0=ALU.add))
    ro(lambda h: h.tensor_copy(r_wrk.ap(), r_msk.ap()))
    for _ in range(3):
        ro(lambda h: h.reduce_max(c32a.ap(), v3(r_wrk), axis=AX.X))
        ro(lambda h: h.tensor_tensor(
            v3(r_eq), v3(r_wrk),
            c32a.ap().rearrange("p (o e) -> p o e", e=1).broadcast_to(B),
            op=ALU.is_ge))
        ro(lambda h: h.scalar_tensor_tensor(v3(r_wrk), v3(r_eq), -1e9, v3(r_wrk),
                                            op0=ALU.mult, op1=ALU.add))
    ro(lambda h: h.reduce_max(c32a.ap(), v3(r_wrk), axis=AX.X))
    ro(lambda h: h.tensor_tensor(
        v3(r_eq), v3(r_msk),
        c32a.ap().rearrange("p (o e) -> p o e", e=1).broadcast_to(B),
        op=ALU.is_ge))
    ro(lambda h: h.tensor_tensor(r_tw.ap(), scores.ap(), r_eq.ap(), op=ALU.mult))
    ro(lambda h: h.reduce_sum(c32a.ap(), v3(r_tw), axis=AX.X))
    ro(lambda h: h.tensor_scalar(c32a.ap(), c32a.ap(), 1e-20, None, op0=ALU.add))
    ro(lambda h: h.reciprocal(c32b.ap(), c32a.ap()))
    tok_comb = vp.on("vector",
                     lambda h: h.scalar_tensor_tensor(
                         v3(comb), v3(r_tw), 2.5,
                         c32b.ap().rearrange("p (o e) -> p o e", e=1).broadcast_to(B),
                         op0=ALU.mult, op1=ALU.mult),
                     inc=s.dve, sync=True)

    if "dbg_scores" in io:
        vp.on("gpsimd", lambda h: h.dma_start(out=io["dbg_scores"], in_=scores[:, :]),
              waits=[(s.dve, s.dve.count), (s.act, s.act.count)], inc=s.dst, amt=16)
    if "dbg_comb" in io:
        vp.on("gpsimd", lambda h: h.dma_start(out=io["dbg_comb"], in_=comb[:, :]),
              waits=[(s.dve, s.dve.count)], inc=s.dst, amt=16)
    vp.barrier(s.bar)
    rstk.close()

    tok_accg = [0] * 8

    ysb_free = [0, 0]

    def expert_ffn(gT, uT, dT, n_ft, f_stride, comb_e, p_buf, silb, ysb_pair,
                   tok_w, accum, w_sem=None, bg_step=None):
        two_ysb = ysb_pair[0] is not ysb_pair[1]
        for g in range(8):
            bslot = (g % 2) if two_ysb else 0
            for ft in range(n_ft):
                bank_h, fr_h, bi_h = hu_rot.next()
                for kc in range(8):
                    vp.on("tensor",
                          lambda h, g=g, ft=ft, kc=kc, bank=bank_h, gT=gT: h.matmul(
                              bank[:, :],
                              gT[:, kc * f_stride + ft * 128:kc * f_stride + (ft + 1) * 128],
                              xT[:, kc * T + g * 512:kc * T + (g + 1) * 512],
                              start=(kc == 0), stop=(kc == 7)),
                          waits=[((w_sem or s.dve), tok_w), fr_h]
                          if kc == 0 else (),
                          inc=s.pe if kc == 7 else None)
                tok_h = s.pe.count
                bank_u, fr_u, bi_u = hu_rot.next()
                for kc in range(8):
                    vp.on("tensor",
                          lambda h, g=g, ft=ft, kc=kc, bank=bank_u, uT=uT: h.matmul(
                              bank[:, :],
                              uT[:, kc * f_stride + ft * 128:kc * f_stride + (ft + 1) * 128],
                              xT[:, kc * T + g * 512:kc * T + (g + 1) * 512],
                              start=(kc == 0), stop=(kc == 7)),
                          waits=[fr_u] if kc == 0 else (),
                          inc=s.pe if kc == 7 else None)
                tok_u = s.pe.count
                tok_sil = vp.on("scalar",
                                lambda h, bank=bank_h: h.activation(
                                    silb[:, :], bank[:, :], AF.Silu),
                                waits=[(s.pe, tok_h), (s.dve, s.dve.count)],
                                inc=s.act)
                hu_rot.set_free(bi_h, s.act, tok_sil)
                tok_p = vp.on("vector",
                              lambda h, ft=ft, bank=bank_u: h.tensor_tensor(
                                  p_buf[:, ft * 512:(ft + 1) * 512], silb[:, :],
                                  bank[:, :], op=ALU.mult),
                              waits=[(s.act, tok_sil), (s.pe, tok_u)], inc=s.dve)
                hu_rot.set_free(bi_u, s.dve, tok_p)
            tok_pall = s.dve.count
            ysb = ysb_pair[bslot]
            for tt in range(4):
                for nh in range(2):
                    bank, fr, bi = y_rot.next()
                    for fc in range(n_ft):
                        vp.on("tensor",
                              lambda h, tt=tt, nh=nh, fc=fc, bank=bank, dT=dT: h.matmul(
                                  bank[:, :],
                                  p_buf[:, fc * 512 + tt * 128:fc * 512 + (tt + 1) * 128],
                                  dT[:, fc * 1024 + nh * 512:fc * 1024 + (nh + 1) * 512],
                                  start=(fc == 0), stop=(fc == n_ft - 1)),
                              waits=([(s.dve, tok_pall), fr,
                                      (s.dst, ysb_free[bslot])]
                                     if fc == 0 else ()),
                              inc=s.pe if fc == n_ft - 1 else None)
                    tok_y = s.pe.count
                    eng = "vector" if (tt * 2 + nh) % 2 == 0 else "scalar"
                    esem = s.dve if eng == "vector" else s.act
                    if comb_e is None:
                        def mk_cp(eng, tt=tt, nh=nh, bank=bank, ysb=ysb):
                            dst = ysb[:, tt * D + nh * 512:tt * D + (nh + 1) * 512]
                            if eng == "vector":
                                return lambda h: h.tensor_copy(dst, bank[:, :])
                            return lambda h: h.activation(dst, bank[:, :], AF.Copy)
                        tok_ev = vp.on(eng, mk_cp(eng),
                                       waits=[(s.pe, tok_y)], inc=esem)
                    else:
                        def mk_ev(eng, g=g, tt=tt, nh=nh, bank=bank, ysb=ysb):
                            cs = comb[:, (g * 4 + tt) * 16 + comb_e:
                                      (g * 4 + tt) * 16 + comb_e + 1]
                            dst = ysb[:, tt * D + nh * 512:tt * D + (nh + 1) * 512]
                            if eng == "vector":
                                return lambda h: h.tensor_scalar(dst, bank[:, :], cs,
                                                                 None, op0=ALU.mult)
                            return lambda h: h.activation(dst, bank[:, :], AF.Copy,
                                                          scale=cs)
                        tok_ev = vp.on(eng, mk_ev(eng),
                                       waits=[(s.pe, tok_y)], inc=esem)
                    y_rot.set_free(bi, esem, tok_ev)
            tok_st = vp.on("gpsimd",
                           lambda h, g=g, ysb=ysb, accum=accum: h.dma_start(
                               out=accr[g],
                               in_=ysb.ap().rearrange("p (tt d) -> p tt d", tt=4),
                               **({"accum_op": ALU.add} if accum else {})),
                           waits=[(s.dve, s.dve.count), (s.act, s.act.count),
                                  (s.dst, tok_accg[g])],
                           inc=s.dst, amt=16)
            ysb_free[bslot] = tok_st
            tok_accg[g] = tok_st
            if bg_step is not None:
                bg_step()

    with nc.sbuf_tensor("d_silb", [128, 512], BF16) as silb, \
         nc.sbuf_tensor("d_ysb1", [128, 4 * D], F32) as ysb1:
        with nc.sbuf_tensor("d_wstage", [128, 2048], F32) as wstage, \
             nc.sbuf_tensor("d_psh", [128, 16 * 512], BF16) as p_buf, \
             nc.sbuf_tensor("d_shg", [128, 8 * 2048], BF16) as shg, \
             nc.sbuf_tensor("d_shu", [128, 8 * 2048], BF16) as shu, \
             nc.sbuf_tensor("d_shd", [128, 16 * 1024], BF16) as shd:
            load_convert(vp, s, io["sh_gate_w"], 1024, 2048, wstage, shg)
            load_convert(vp, s, io["sh_up_w"], 1024, 2048, wstage, shu)
            load_convert(vp, s, io["sh_down_w"], 2048, 1024, wstage, shd)
            tok_w = s.dve.count
            vp.wait("tensor", s.dve, tok_comb)
            expert_ffn(shg, shu, shd, 16, 2048, None, p_buf, silb, [ysb1, ysb1],
                       tok_w, accum=False)
            vp.barrier(s.bar)

        with nc.sbuf_tensor("d_wstage2", [128, 1024], F32) as wstage, \
             nc.sbuf_tensor("d_psh2", [128, 8 * 512], BF16) as p_buf, \
             nc.sbuf_tensor("d_gweb0", [128, 8 * 1024], BF16) as gweb0, \
             nc.sbuf_tensor("d_uweb0", [128, 8 * 1024], BF16) as uweb0, \
             nc.sbuf_tensor("d_dweb0", [128, 8 * 1024], BF16) as dweb0, \
             nc.sbuf_tensor("d_gweb1", [128, 8 * 1024], BF16) as gweb1, \
             nc.sbuf_tensor("d_uweb1", [128, 8 * 1024], BF16) as uweb1, \
             nc.sbuf_tensor("d_dweb1", [128, 8 * 1024], BF16) as dweb1, \
             nc.sbuf_tensor("d_ysb2", [128, 4 * D], F32) as ysb2:
            wsets = [(gweb0, uweb0, dweb0), (gweb1, uweb1, dweb1)]
            wfree = [0, 0]
            wstages = [wstage, wstage]
            bgq = []

            def queue_expert(e):
                gw, uw, dw = wsets[e % 2]
                pg = lambda e=e: wfree[e % 2]
                for w_hbm, dst in ((io["gate_w"][e], gw), (io["up_w"][e], uw),
                                   (io["down_w"][e], dw)):
                    bgq.extend(load_convert_bg(vp, s, w_hbm, 1024, 1024,
                                               wstages, dst, pe_gate=pg))

            def bg_step():
                n = min(3, len(bgq))
                for f in bgq[:n]:
                    f()
                del bgq[:n]

            queue_expert(0)
            while bgq:
                bg_step()
            for e in range(E):
                if e + 1 < E:
                    queue_expert(e + 1)
                wready = s.pool.count
                expert_ffn(*wsets[e % 2], 8, 1024, e, p_buf, silb,
                           [ysb1, ysb2], wready, accum=True,
                           w_sem=s.pool, bg_step=bg_step)
                wfree[e % 2] = s.pe.count
        vp.barrier(s.bar)

    with nc.sbuf_tensor("d_sc", [128, 4 * D], F32) as sc, \
         nc.sbuf_tensor("d_accsb", [128, 4 * D], F32) as accsb, \
         nc.sbuf_tensor("d_sqb", [128, D], F32) as sqb:
        for tb in range(8):
            tok_ld = vp.on("sync",
                           lambda h, tb=tb: h.dma_start(
                               out=sc.ap().rearrange("p (tt d) -> p tt d", tt=4),
                               in_=s3r[tb]),
                           waits=[(s.dst, s.dst.count)],
                           inc=s.dld, amt=16)
            tok_lda = vp.on("sync",
                            lambda h, tb=tb: h.dma_start(
                                out=accsb.ap().rearrange("p (tt d) -> p tt d", tt=4),
                                in_=accr[tb]),
                            inc=s.dld, amt=16)
            vp.on("vector",
                  lambda h: h.tensor_tensor(sc[:, :], sc[:, :], accsb[:, :],
                                            op=ALU.add),
                  waits=[(s.dld, tok_lda)], inc=s.dve)
            for tt in range(4):
                tok_ln = layernorm_inplace(vp, s, sc[:, tt * D:(tt + 1) * D], sqb)
            vp.on("gpsimd",
                  lambda h, tb=tb: h.dma_start(
                      out=outr[tb],
                      in_=sc.ap().rearrange("p (tt d) -> p tt d", tt=4)),
                  waits=[(s.dve, tok_ln)], inc=s.dst, amt=16)

    vp.wait("gpsimd", s.dst, s.dst.count)
    return stk



def build_fused(nc, vp, io):
    gbar = vp.sem("gbar")
    stk = build_phase_a(nc, vp, io)
    vp.barrier(gbar)
    stk.close()
    stk = build_phase_b(nc, vp, io)
    vp.barrier(gbar)
    stk.close()
    stk = build_phase_c(nc, vp, io)
    vp.barrier(gbar)
    stk.close()
    stk = build_phase_d(nc, vp, io)
    return stk



def _dram_in(nc, name, shape):
    return nc.dram_tensor(name, list(shape), F32, kind="ExternalInput").ap()


def _run_phase(build, io_spec, in_maps, out_key):
    nc = bass.Bass("TRN2", target_bir_lowering=False)
    io = {}
    for key, kind, shape in io_spec:
        io[key] = nc.dram_tensor("t_" + key, list(shape), F32, kind=kind).ap()
    vp = VP(nc)
    stk = build(nc, vp, io)
    with stk:
        vp.run()
    res = run_bass_kernel_spmd(nc, in_maps, core_ids=list(range(8)))
    return [np.asarray(r["t_" + out_key]) for r in res.results]


def kernel(**inputs):
    g = lambda k: np.ascontiguousarray(np.asarray(inputs[k], dtype=np.float32))
    x = g("L0_attn_x")
    gate = g("L0_ffn_gate").reshape(1, 1)

    ins = [("x", (T, D)),
           ("Wq", (D, D)), ("bq", (D,)), ("Wk", (D, D)), ("bk", (D,)),
           ("Wv", (D, D)), ("bv", (D,)), ("Wo", (D, D)), ("bo", (D,)),
           ("w_fc", (4096, D)), ("w_proj", (D, 4096)), ("u", (4096, 16)),
           ("v", (16, D)), ("fast_b", (16, 16)), ("gate", (1, 1)),
           ("Wqkv", (D, 3 * D)), ("Wout", (D, D)), ("head_scales", (8,)),
           ("head_dirs", (8, 3)),
           ("router_w", (16, D)), ("bias", (16,)), ("gate_w", (16, D, D)),
           ("up_w", (16, D, D)), ("down_w", (16, D, D)),
           ("sh_gate_w", (D, 2048)), ("sh_up_w", (D, 2048)),
           ("sh_down_w", (2048, D))]

    nc = bass.Bass("TRN2", target_bir_lowering=False)
    io = {}
    for key, shape in ins:
        io[key] = nc.dram_tensor("t_" + key, list(shape), F32,
                                 kind="ExternalInput").ap()
    for key in ("s1", "s2", "s3"):
        io[key] = nc.dram_tensor("t_" + key, [T, D], F32, kind="Internal").ap()
    io["out"] = nc.dram_tensor("t_out", [T, D], F32, kind="ExternalOutput").ap()

    vp = VP(nc)
    stk = build_fused(nc, vp, io)
    with stk:
        vp.run()

    com = dict(t_Wq=g("L0_attn_Wq"), t_bq=g("L0_attn_bq"),
               t_Wk=g("L0_attn_Wk"), t_bk=g("L0_attn_bk"),
               t_Wv=g("L0_attn_Wv"), t_bv=g("L0_attn_bv"),
               t_Wo=g("L0_attn_Wo"), t_bo=g("L0_attn_bo"),
               t_w_fc=g("L0_ffn_w_fc"), t_w_proj=g("L0_ffn_w_proj"),
               t_u=g("L0_ffn_u"), t_v=g("L0_ffn_v"),
               t_fast_b=g("L0_ffn_fast_b"), t_gate=gate,
               t_Wqkv=g("L1_attn_Wqkv"), t_Wout=g("L1_attn_Wout"),
               t_head_scales=g("L1_attn_head_scales"),
               t_head_dirs=g("L1_attn_head_dirs"),
               t_router_w=g("L1_ffn_router_w"), t_bias=g("L1_ffn_bias"),
               t_gate_w=g("L1_ffn_gate_w"), t_up_w=g("L1_ffn_up_w"),
               t_down_w=g("L1_ffn_down_w"), t_sh_gate_w=g("L1_ffn_sh_gate_w"),
               t_sh_up_w=g("L1_ffn_sh_up_w"), t_sh_down_w=g("L1_ffn_sh_down_w"))
    maps = [dict(t_x=x[b], **com) for b in range(8)]
    res = run_bass_kernel_spmd(nc, maps, core_ids=list(range(8)))
    return np.stack([np.asarray(r["t_out"]) for r in res.results],
                    axis=0).astype(np.float32)


# version 29 (final)
# speedup vs baseline: 1.0602x; 1.0602x over previous
import math
from contextlib import ExitStack

import numpy as np
import concourse.bass as bass
import concourse.mybir as mybir
from concourse.bass_utils import run_bass_kernel_spmd

F32 = mybir.dt.float32
BF16 = mybir.dt.bfloat16
AF = mybir.ActivationFunctionType
ALU = mybir.AluOpType
AX = mybir.AxisListType

T = 4096
D = 1024
ENGS = ("tensor", "vector", "scalar", "gpsimd", "sync")

MAGIC = 12582912.0


class Sem:
    __slots__ = ("name", "handle", "count")

    def __init__(self, name):
        self.name = name
        self.handle = None
        self.count = 0


class VP:

    def __init__(self, nc):
        self.nc = nc
        self.thunks = {e: [] for e in ENGS}
        self.sems = []
        self.selfs = {e: self.sem("self_" + e) for e in ENGS}

    def sem(self, name):
        s = Sem(name)
        self.sems.append(s)
        return s

    def on(self, eng, fn, waits=(), inc=None, amt=1, sync=False):
        flat = []
        for w in waits:
            if not w:
                continue
            if isinstance(w, list):
                flat.extend(x for x in w if x)
            else:
                flat.append(w)
        waits = tuple(flat)
        if sync and inc is None:
            inc = self.selfs[eng]
        stok = inc.count + amt if (sync and inc is not None) else 0

        def thunk(h, fn=fn, waits=waits, inc=inc, amt=amt, sync=sync, stok=stok):
            for (s, v) in waits:
                if v > 0:
                    h.wait_ge(s.handle, v)
            ins = fn(h)
            if inc is not None:
                ins.then_inc(inc.handle, amt)
            if sync:
                h.wait_ge(inc.handle, stok)

        self.thunks[eng].append(thunk)
        if inc is not None:
            inc.count += amt
            return inc.count
        return None

    def wait(self, eng, sem, val):
        def thunk(h, sem=sem, val=val):
            if val > 0:
                h.wait_ge(sem.handle, val)

        self.thunks[eng].append(thunk)

    def barrier(self, bar):
        target = bar.count + len(ENGS)
        for e in ENGS:
            def thunk(h, bar=bar, target=target):
                h.sem_inc(bar.handle, 1)
                h.wait_ge(bar.handle, target)

            self.thunks[e].append(thunk)
        bar.count = target

    def run(self):
        nc = self.nc
        with ExitStack() as stk:
            for s in self.sems:
                s.handle = stk.enter_context(nc.semaphore(s.name))
            with nc.Block() as blk:
                def runner(eng):
                    def go(h, eng=eng):
                        for t in self.thunks[eng]:
                            t(h)
                    return go

                blk.tensor(runner("tensor"))
                blk.vector(runner("vector"))
                blk.scalar(runner("scalar"))
                blk.gpsimd(runner("gpsimd"))
                blk.sync(runner("sync"))


class PsRot:

    def __init__(self, banks):
        self.banks = banks
        self.free = [None] * len(banks)
        self.i = 0

    def next(self):
        i = self.i % len(self.banks)
        self.i += 1
        return self.banks[i], self.free[i], i

    def set_free(self, i, sem, tok):
        self.free[i] = (sem, tok)

    def set_free2(self, i, *gates):
        self.free[i] = list(gates)


def make_identity(vp, ident, inc=None):
    vp.on("gpsimd", lambda h, a=ident: h.memset(a[:, :], 0.0))
    n = ident.shape[0]
    return vp.on(
        "gpsimd",
        lambda h, a=ident, n=n: h.affine_select(
            out=a[:, :], in_=a[:, :], compare_op=ALU.not_equal, fill=1.0,
            base=0, pattern=[[-1, n]], channel_multiplier=1),
        inc=inc,
    )


def layernorm_inplace(vp, s, x_tile, sqb, width=D, eps=1e-5):
    vp.on("vector", lambda h: h.reduce_sum(s.col0[:, 0:1], x_tile, axis=AX.X),
          sync=True)
    vp.on("vector", lambda h: h.tensor_scalar(s.col0[:, 0:1], s.col0[:, 0:1],
                                              -1.0 / width, None, op0=ALU.mult),
          sync=True)
    vp.on("vector", lambda h: h.tensor_scalar(x_tile, x_tile, s.col0[:, 0:1], None,
                                              op0=ALU.add), sync=True)
    vp.on("vector", lambda h: h.tensor_tensor(sqb[:, 0:width], x_tile, x_tile,
                                              op=ALU.mult), sync=True)
    vp.on("vector", lambda h: h.reduce_sum(s.col1[:, 0:1], sqb[:, 0:width], axis=AX.X),
          sync=True)
    tok = vp.on("vector", lambda h: h.tensor_scalar(s.col1[:, 0:1], s.col1[:, 0:1],
                                                    1.0 / width, eps, op0=ALU.mult,
                                                    op1=ALU.add), inc=s.dve, sync=True)
    vp.on("scalar", lambda h: h.activation(s.col2[:, 0:1], s.col1[:, 0:1], AF.Ln),
          waits=[(s.dve, tok)], sync=True)
    tok3 = vp.on("scalar", lambda h: h.activation(s.col2[:, 0:1], s.col2[:, 0:1],
                                                  AF.Exp, scale=-0.5), inc=s.act,
                 sync=True)
    tok4 = vp.on("vector", lambda h: h.tensor_scalar(x_tile, x_tile, s.col2[:, 0:1],
                                                     None, op0=ALU.mult),
                 waits=[(s.act, tok3)], inc=s.dve)
    return tok4


class NS:
    pass


def mk_sems(vp, pfx):
    s = NS()
    s.dld = vp.sem(pfx + "_dld")
    s.dst = vp.sem(pfx + "_dst")
    s.pe = vp.sem(pfx + "_pe")
    s.dve = vp.sem(pfx + "_dve")
    s.act = vp.sem(pfx + "_act")
    s.pool = vp.sem(pfx + "_pool")
    s.bar = vp.sem(pfx + "_bar")
    return s



def ternarize_and_transpose(vp, s, w_hbm, rows, cols, wstages, wT_dst,
                            ident_bf, tr_rot, wst_free):
    n_rt = rows // 128
    n_ct = cols // 128
    for rt in range(n_rt):
        wstage = wstages[rt % 2]
        tok_ld = vp.on("sync",
                       lambda h, rt=rt, wstage=wstage: h.dma_start(
                           out=wstage[:, 0:cols],
                           in_=w_hbm[rt * 128:(rt + 1) * 128, :]),
                       waits=[wst_free[rt % 2]],
                       inc=s.dld, amt=16)
        vp.on("vector",
              lambda h, wstage=wstage: h.reduce_sum(
                  s.col0[:, 0:1], wstage[:, 0:cols], axis=AX.X,
                  apply_absolute_value=True),
              waits=[(s.dld, tok_ld)], sync=True)
        vp.on("vector", lambda h: h.tensor_scalar(s.col1[:, 0:1], s.col0[:, 0:1],
                                                  1.0 / cols, 1e-5, op0=ALU.mult,
                                                  op1=ALU.max), sync=True)
        vp.on("vector", lambda h: h.reciprocal(s.col2[:, 0:1], s.col1[:, 0:1]),
              sync=True)
        vp.on("vector", lambda h, wstage=wstage: h.tensor_scalar(
            wstage[:, 0:cols], wstage[:, 0:cols], s.col2[:, 0:1], None,
            op0=ALU.mult), sync=True)
        vp.on("vector", lambda h, wstage=wstage: h.tensor_scalar(
            wstage[:, 0:cols], wstage[:, 0:cols], MAGIC, MAGIC, op0=ALU.add,
            op1=ALU.subtract), sync=True)
        vp.on("vector", lambda h, wstage=wstage: h.tensor_scalar(
            wstage[:, 0:cols], wstage[:, 0:cols], 1.0, -1.0, op0=ALU.min,
            op1=ALU.max), sync=True)
        tok_w = vp.on("vector",
                      lambda h, wstage=wstage: h.tensor_scalar(
                          wstage[:, 0:cols], wstage[:, 0:cols],
                          s.col1[:, 0:1], None, op0=ALU.mult),
                      inc=s.dve)
        for ct in range(n_ct):
            bank, fr, bi = tr_rot.next()
            tok_tr = vp.on("tensor",
                           lambda h, ct=ct, bank=bank, wstage=wstage: h.transpose(
                               bank[:, :], wstage[:, ct * 128:(ct + 1) * 128],
                               ident_bf[:, :]),
                           waits=[(s.dve, tok_w), fr], inc=s.pe)
            tok_e = vp.on("vector",
                          lambda h, ct=ct, rt=rt, bank=bank: h.tensor_copy(
                              wT_dst[:, ct * rows + rt * 128: ct * rows + (rt + 1) * 128],
                              bank[:, :]),
                          waits=[(s.pe, tok_tr)], inc=s.dve)
            tr_rot.set_free(bi, s.dve, tok_e)
        wst_free[rt % 2] = (s.pe, s.pe.count)


def build_phase_b(nc, vp, io):
    s = mk_sems(vp, "b")

    stk = ExitStack()
    sb = lambda name, shape, dt: stk.enter_context(nc.sbuf_tensor(name, shape, dt))
    psb = lambda name, shape, dt=F32: stk.enter_context(nc.psum_tensor(name, shape, dt))

    wfcT = sb("b_wfcT", [128, 8 * 4096], BF16)
    wprojT = sb("b_wprojT", [128, 32 * 1024], BF16)
    vT = sb("b_vT", [128, 8 * 16], BF16)
    uT = sb("b_uT", [16, 4096], BF16)
    fbTg = sb("b_fbTg", [16, 16], BF16)
    ident32b = sb("b_ident32b", [128, 128], F32)
    ones_row = sb("b_ones", [1, 16], F32)
    s.col0 = sb("b_col0", [128, 1], F32)
    s.col1 = sb("b_col1", [128, 1], F32)
    s.col2 = sb("b_col2", [128, 1], F32)
    ps_tr1 = psb("b_pstr1", [128, 128], F32)
    ps_tr2 = psb("b_pstr2", [128, 128], F32)
    ps_a = psb("b_psa", [128, 512])
    ps_b = psb("b_psb", [128, 512])
    ps_c = psb("b_psc", [128, 512])
    ps_d = psb("b_psd", [128, 512])
    ps_s1 = psb("b_pss1", [16, 512])
    tr_rot = PsRot([ps_tr1, ps_tr2])
    h_rot = PsRot([ps_a, ps_b])
    y_rot = PsRot([ps_c, ps_d])

    make_identity(vp, ident32b, inc=s.pool)
    vp.on("gpsimd", lambda h: h.memset(ones_row[:, :], 1.0), inc=s.pool)

    vp.wait("tensor", s.pool, 1)

    with nc.sbuf_tensor("b_wstage", [128, 4096], F32) as wstage, \
         nc.sbuf_tensor("b_wstage2", [128, 4096], F32) as wstage2:
        wsts = [wstage, wstage2]
        wstf = [None, None]
        ternarize_and_transpose(vp, s, io["w_fc"], 4096, 1024, wsts,
                                wfcT, ident32b, tr_rot, wstf)
        ternarize_and_transpose(vp, s, io["w_proj"], 1024, 4096, wsts,
                                wprojT, ident32b, tr_rot, wstf)

        tok = vp.on("sync", lambda h: h.dma_start(out=wstage[0:16, 0:1024],
                                                  in_=io["v"]),
                    waits=[(s.pe, s.pe.count)], inc=s.dld, amt=16)
        for j in range(8):
            bank, fr, bi = tr_rot.next()
            tok_tr = vp.on("tensor",
                           lambda h, j=j, bank=bank: h.transpose(
                               bank[:, 0:16], wstage[0:16, j * 128:(j + 1) * 128],
                               ident32b[0:16, 0:16]),
                           waits=[(s.dld, tok), fr], inc=s.pe)
            tok_e = vp.on("vector",
                          lambda h, j=j, bank=bank: h.tensor_copy(
                              vT[:, j * 16:(j + 1) * 16], bank[:, 0:16]),
                          waits=[(s.pe, tok_tr)], inc=s.dve)
            tr_rot.set_free(bi, s.dve, tok_e)
        tok = vp.on("sync", lambda h: h.dma_start(
            out=wstage2[:, 0:512].rearrange("p (rt r) -> p rt r", r=16),
            in_=io["u"].rearrange("(rt p) r -> p rt r", p=128)),
                    waits=[(s.pe, s.pe.count)], inc=s.dld, amt=16)
        for rt in range(32):
            bank, fr, bi = tr_rot.next()
            tok_tr = vp.on("tensor",
                           lambda h, rt=rt, bank=bank: h.transpose(
                               bank[0:16, 0:128], wstage2[:, rt * 16:(rt + 1) * 16],
                               ident32b[:, :]),
                           waits=[(s.dld, tok), fr], inc=s.pe)
            tok_e = vp.on("vector",
                          lambda h, rt=rt, bank=bank: h.tensor_copy(
                              uT[0:16, rt * 128:(rt + 1) * 128], bank[0:16, 0:128]),
                          waits=[(s.pe, tok_tr)], inc=s.dve)
            tr_rot.set_free(bi, s.dve, tok_e)
        tok = vp.on("sync", lambda h: h.dma_start(out=wstage[0:16, 0:16],
                                                  in_=io["fast_b"]),
                    waits=[(s.pe, s.pe.count)], inc=s.dld, amt=16)
        tokg = vp.on("sync", lambda h: h.dma_start(out=wstage[0:1, 16:17],
                                                   in_=io["gate"]),
                     inc=s.dld, amt=16)
        tok_g = vp.on("tensor", lambda h: h.matmul(ps_s1[:, 0:1], ones_row[0:1, 0:16],
                                                   wstage[0:1, 16:17], start=True,
                                                   stop=True),
                      waits=[(s.dld, tokg), (s.pool, 2)], inc=s.pe)
        tok_gc = vp.on("vector", lambda h: h.tensor_copy(s.col0[0:16, 0:1],
                                                         ps_s1[0:16, 0:1]),
                       waits=[(s.pe, tok_g)], inc=s.dve)
        tok_tr = vp.on("tensor", lambda h: h.transpose(ps_s1[0:16, 0:16],
                                                       wstage[0:16, 0:16],
                                                       ident32b[0:16, 0:16]),
                       waits=[(s.dld, tok), (s.dve, tok_gc)], inc=s.pe)
        vp.on("vector", lambda h: h.tensor_scalar(fbTg[0:16, 0:16], ps_s1[0:16, 0:16],
                                                  s.col0[0:16, 0:1], None, op0=ALU.mult),
              waits=[(s.pe, tok_tr)], inc=s.dve)

        vp.barrier(s.bar)

    def dbg(name, ap_src):
        if name in io:
            vp.on("gpsimd", lambda h, name=name, ap_src=ap_src: h.dma_start(
                out=io[name], in_=ap_src),
                waits=[(s.dve, s.dve.count), (s.act, s.act.count)],
                inc=s.dst, amt=16)

    dbg("dbg_wfcT", wfcT[:, :])
    dbg("dbg_wprojT", wprojT[:, :])
    dbg("dbg_vT", vT[:, :])
    dbg("dbg_uT", uT[:, :])
    dbg("dbg_fbTg", fbTg[:, :])

    sc = sb("b_sc", [128, 4 * D], F32)
    sc2 = sc
    xTb = sb("b_xTb", [128, 8 * 512], BF16)
    l1b = sb("b_l1b", [16, 512], BF16)
    l2b = sb("b_l2b", [16, 512], BF16)
    rbuf = sb("b_rbuf", [128, 512], BF16)
    r2sb = sb("b_r2", [128, 32 * 512], BF16)
    sqb = xTb.ap().bitcast(F32)

    s1r = io["s1"].rearrange("(tb tt p) d -> tb p tt d", p=128, tt=4)
    s2r = io["s2"].rearrange("(tb tt p) d -> tb p tt d", p=128, tt=4)

    sc_store_tok = [0, 0]
    for tb in range(8):
        sc_t = sc
        tok_ld = vp.on("sync",
                       lambda h, tb=tb, sc_t=sc_t: h.dma_start(
                           out=sc_t.ap().rearrange("p (tt d) -> p tt d", tt=4),
                           in_=s1r[tb]),
                       waits=[(s.dst, max(sc_store_tok))],
                       inc=s.dld, amt=16)
        for tt in range(4):
            for j in range(8):
                bank, fr, bi = tr_rot.next()
                tok_tr = vp.on("tensor",
                               lambda h, tt=tt, j=j, bank=bank, sc_t=sc_t:
                               h.transpose(
                                   bank[:, :],
                                   sc_t[:, tt * D + j * 128:tt * D + (j + 1) * 128],
                                   ident32b[:, :]),
                               waits=[(s.dld, tok_ld), fr], inc=s.pe)
                tok_e = vp.on("vector",
                              lambda h, tt=tt, j=j, bank=bank: h.tensor_copy(
                                  xTb[:, j * 512 + tt * 128:j * 512 + (tt + 1) * 128],
                                  bank[:, :]),
                              waits=[(s.pe, tok_tr)], inc=s.dve)
                tr_rot.set_free(bi, s.dve, tok_e)
        tok_xT = s.dve.count

        for kc in range(8):
            vp.on("tensor",
                  lambda h, kc=kc: h.matmul(ps_s1[:, :], vT[:, kc * 16:(kc + 1) * 16],
                                            xTb[:, kc * 512:(kc + 1) * 512],
                                            start=(kc == 0), stop=(kc == 7)),
                  waits=[(s.dve, tok_xT)] if kc == 0 else (),
                  inc=s.pe if kc == 7 else None)
        tok_l1 = vp.on("vector", lambda h: h.tensor_copy(l1b[:, :], ps_s1[0:16, :]),
                       waits=[(s.pe, s.pe.count)], inc=s.dve)
        vp.on("tensor", lambda h: h.matmul(ps_s1[0:16, 0:512], fbTg[:, :], l1b[:, :],
                                           start=True, stop=True),
              waits=[(s.dve, tok_l1)], inc=s.pe)
        tok_l2 = vp.on("vector", lambda h: h.tensor_copy(l2b[:, :], ps_s1[0:16, :]),
                       waits=[(s.pe, s.pe.count)], inc=s.dve)

        for hc in range(32):
            bank, fr, bi = h_rot.next()
            for kc in range(8):
                vp.on("tensor",
                      lambda h, hc=hc, kc=kc, bank=bank: h.matmul(
                          bank[:, :],
                          wfcT[:, kc * 4096 + hc * 128:kc * 4096 + (hc + 1) * 128],
                          xTb[:, kc * 512:(kc + 1) * 512],
                          start=(kc == 0), stop=False),
                      waits=([(s.dve, tok_l2), fr] if kc == 0 else ()))
            tok_h = vp.on("tensor",
                          lambda h, hc=hc, bank=bank: h.matmul(
                              bank[:, :], uT[:, hc * 128:(hc + 1) * 128], l2b[:, :],
                              start=False, stop=True),
                          inc=s.pe)
            tok_r = vp.on("scalar",
                          lambda h, bank=bank: h.activation(rbuf[:, :], bank[:, :],
                                                            AF.Relu),
                          waits=[(s.pe, tok_h), (s.dve, s.dve.count)], inc=s.act)
            h_rot.set_free(bi, s.act, tok_r)
            vp.on("vector",
                  lambda h, hc=hc: h.tensor_tensor(r2sb[:, hc * 512:(hc + 1) * 512],
                                                   rbuf[:, :], rbuf[:, :], op=ALU.mult),
                  waits=[(s.act, tok_r)], inc=s.dve)

        for tt in range(4):
            for nh in range(2):
                bank, fr, bi = y_rot.next()
                for hc in range(32):
                    vp.on("tensor",
                          lambda h, tt=tt, nh=nh, hc=hc, bank=bank: h.matmul(
                              bank[:, :],
                              r2sb[:, hc * 512 + tt * 128:hc * 512 + (tt + 1) * 128],
                              wprojT[:, hc * 1024 + nh * 512:hc * 1024 + (nh + 1) * 512],
                              start=(hc == 0), stop=(hc == 31)),
                          waits=([(s.dve, s.dve.count), fr] if hc == 0 else ()),
                          inc=s.pe if hc == 31 else None)
                tok_y = s.pe.count
                tok_res = vp.on("vector",
                                lambda h, tt=tt, nh=nh, bank=bank, sc_t=sc_t:
                                h.tensor_tensor(
                                    sc_t[:, tt * D + nh * 512:tt * D + (nh + 1) * 512],
                                    bank[:, :],
                                    sc_t[:, tt * D + nh * 512:tt * D + (nh + 1) * 512],
                                    op=ALU.add),
                                waits=[(s.pe, tok_y)], inc=s.dve)
                y_rot.set_free(bi, s.dve, tok_res)

        if tb == 0:
            dbg("dbg_xTb", xTb[:, :])
            dbg("dbg_l1b", l1b[:, :])
            dbg("dbg_l2b", l2b[:, :])
            dbg("dbg_r2", r2sb[:, :])
            dbg("dbg_res", sc[:, :])

        for tt in range(4):
            tok_ln = layernorm_inplace(vp, s, sc_t[:, tt * D:(tt + 1) * D], sqb)
        sc_store_tok[tb % 2] = vp.on(
            "gpsimd",
            lambda h, tb=tb, sc_t=sc_t: h.dma_start(
                out=s2r[tb],
                in_=sc_t.ap().rearrange("p (tt d) -> p tt d", tt=4)),
            waits=[(s.dve, tok_ln)], inc=s.dst, amt=16)

    vp.wait("gpsimd", s.dst, s.dst.count)
    return stk




def transpose_weight(vp, s, w_hbm, rows, cols, wstages, weffs, dst, ident_bf,
                     tr_rot, wst_free, weff_free):
    n_rt = rows // 128
    n_ct = cols // 128
    for rt in range(n_rt):
        wstage = wstages[rt % 2]
        weff = weffs[rt % 2]
        tok_ld = vp.on("sync",
                       lambda h, rt=rt, wstage=wstage: h.dma_start(
                           out=wstage[:, 0:cols],
                           in_=w_hbm[rt * 128:(rt + 1) * 128, :]),
                       waits=[wst_free[rt % 2]],
                       inc=s.dld, amt=16)
        tok_w = vp.on("vector",
                      lambda h, wstage=wstage, weff=weff: h.tensor_copy(
                          weff[:, 0:cols], wstage[:, 0:cols]),
                      waits=[(s.dld, tok_ld), weff_free[rt % 2]],
                      inc=s.dve)
        wst_free[rt % 2] = (s.dve, tok_w)
        for ct in range(n_ct):
            bank, fr, bi = tr_rot.next()
            bb = bank.ap().bitcast(BF16)[:, 0:128]
            tok_tr = vp.on("tensor",
                           lambda h, ct=ct, bb=bb, weff=weff: h.transpose(
                               bb, weff[:, ct * 128:(ct + 1) * 128],
                               ident_bf[:, :]),
                           waits=[(s.dve, tok_w), fr], inc=s.pe)
            tok_e = vp.on("vector",
                          lambda h, ct=ct, rt=rt, bb=bb: h.tensor_copy(
                              dst[:, ct * rows + rt * 128: ct * rows + (rt + 1) * 128],
                              bb),
                          waits=[(s.pe, tok_tr)], inc=s.dve)
            tr_rot.set_free(bi, s.dve, tok_e)
        weff_free[rt % 2] = (s.pe, s.pe.count)


def build_phase_a(nc, vp, io):
    s = mk_sems(vp, "a")
    H, DH = 16, 64
    C = 512
    NCH = T // C

    stk = ExitStack()
    sb = lambda name, shape, dt: stk.enter_context(nc.sbuf_tensor(name, shape, dt))
    psb = lambda name, shape, dt=F32: stk.enter_context(nc.psum_tensor(name, shape, dt))

    wqT = sb("a_wqT", [128, 8 * 1024], BF16)
    wkT = sb("a_wkT", [128, 8 * 1024], BF16)
    wvT = sb("a_wvT", [128, 8 * 1024], BF16)
    woT = sb("a_woT", [128, 8 * 1024], BF16)
    bqcol = sb("a_bqcol", [64, 16], F32)
    bkcol = sb("a_bkcol", [64, 16], F32)
    nbqcol = sb("a_nbqcol", [64, 16], F32)
    nbkcol = sb("a_nbkcol", [64, 16], F32)
    bv_bc = sb("a_bvbc", [128, 1024], F32)
    bo_bc = sb("a_bobc", [128, 1024], F32)
    identb = sb("a_identb", [128, 128], BF16)
    ident32a = sb("a_ident32a", [128, 128], F32)
    ones_row = sb("a_ones", [1, 128], F32)
    masks = sb("a_masks", [128, 4 * 512], F32)
    Sz = sb("a_Sz", [64, H * 65], F32)
    Szb = sb("a_Szb", [64, H * 65], BF16)
    s.col0 = sb("a_col0", [128, 1], F32)
    s.col1 = sb("a_col1", [128, 1], F32)
    s.col2 = sb("a_col2", [128, 1], F32)


    ps_tr1 = psb("a_pstr1", [128, 128], F32)
    ps_tr2 = psb("a_pstr2", [128, 128], F32)
    ps_a = psb("a_psa", [128, 512])
    ps_b = psb("a_psb", [128, 512])
    ps_n1 = psb("a_psn1", [128, 65])
    ps_n2 = psb("a_psn2", [128, 65])
    ps_st = psb("a_psst", [64, 65])
    ps_c2 = psb("a_psc2", [128, 512])
    tr_rot = PsRot([ps_tr1, ps_tr2])
    big_rot = PsRot([ps_a, ps_b, ps_c2])
    nd_rot = PsRot([ps_n1, ps_n2])

    make_identity(vp, identb)
    make_identity(vp, ident32a)
    vp.on("gpsimd", lambda h: h.memset(ones_row[:, :], 1.0))
    vp.on("gpsimd", lambda h: h.memset(Sz[:, :], 0.0))
    vp.on("gpsimd", lambda h: h.memset(Szb[:, :], 0.0))
    vp.on("gpsimd", lambda h: h.memset(vz[:, :], 1.0))
    for cki in range(4):
        vp.on("gpsimd", lambda h, cki=cki: h.memset(masks[:, cki * 512:(cki + 1) * 512], 1.0))
        vp.on("gpsimd",
              lambda h, cki=cki: h.affine_select(
                  out=masks[:, cki * 512:(cki + 1) * 512],
                  in_=masks[:, cki * 512:(cki + 1) * 512],
                  compare_op=ALU.is_ge, fill=0.0, base=-cki * 128,
                  pattern=[[1, 512]], channel_multiplier=-1),
              inc=None)
    vp.on("gpsimd", lambda h: h.engine_nop(), inc=s.pool)

    vp.wait("tensor", s.pool, 1)

    with nc.sbuf_tensor("a_wstage", [128, 1024], F32) as wstage, \
         nc.sbuf_tensor("a_wstage2", [128, 1024], F32) as wstage2, \
         nc.sbuf_tensor("a_weff", [128, 1024], BF16) as weff, \
         nc.sbuf_tensor("a_weff2", [128, 1024], BF16) as weff2:
        wsts, wefs = [wstage, wstage2], [weff, weff2]
        wstf, weft = [None, None], [None, None]
        transpose_weight(vp, s, io["Wq"], 1024, 1024, wsts, wefs, wqT, identb,
                         tr_rot, wstf, weft)
        transpose_weight(vp, s, io["Wk"], 1024, 1024, wsts, wefs, wkT, identb,
                         tr_rot, wstf, weft)
        transpose_weight(vp, s, io["Wv"], 1024, 1024, wsts, wefs, wvT, identb,
                         tr_rot, wstf, weft)
        transpose_weight(vp, s, io["Wo"], 1024, 1024, wsts, wefs, woT, identb,
                         tr_rot, wstf, weft)

        for h_ in range(16):
            vp.on("sync",
                  lambda hh, h_=h_: hh.dma_start(
                      out=bqcol[:, h_:h_ + 1],
                      in_=io["bq"].rearrange("(a b) -> a b", b=1)[h_ * 64:(h_ + 1) * 64]),
                  inc=s.dld, amt=16)
            vp.on("sync",
                  lambda hh, h_=h_: hh.dma_start(
                      out=bkcol[:, h_:h_ + 1],
                      in_=io["bk"].rearrange("(a b) -> a b", b=1)[h_ * 64:(h_ + 1) * 64]),
                  inc=s.dld, amt=16)
        tok_bv = vp.on("sync", lambda h: h.dma_start(
            out=wstage[0:1, 0:1024], in_=io["bv"].rearrange("(a d) -> a d", a=1)),
            waits=[(s.dve, s.dve.count)], inc=s.dld, amt=16)

        vp.on("vector", lambda h: h.tensor_scalar(nbqcol[:, :], bqcol[:, :], -1.0,
                                                  None, op0=ALU.mult),
              waits=[(s.dld, s.dld.count)])
        vp.on("vector", lambda h: h.tensor_scalar(nbkcol[:, :], bkcol[:, :], -1.0,
                                                  None, op0=ALU.mult), inc=s.dve)
        for half in range(2):
            bank, fr, bi = big_rot.next()
            tk = vp.on("tensor",
                       lambda h, half=half, bank=bank: h.matmul(
                           bank[:, :], ones_row[0:1, :],
                           wstage[0:1, half * 512:(half + 1) * 512],
                           start=True, stop=True),
                       waits=[(s.dld, tok_bv), fr], inc=s.pe)
            te = vp.on("vector",
                       lambda h, half=half, bank=bank: h.tensor_copy(
                           bv_bc[:, half * 512:(half + 1) * 512], bank[:, :]),
                       waits=[(s.pe, tk)], inc=s.dve)
            big_rot.set_free(bi, s.dve, te)
        tok_bo = vp.on("sync", lambda h: h.dma_start(
            out=wstage[0:1, 0:1024], in_=io["bo"].rearrange("(a d) -> a d", a=1)),
            waits=[(s.pe, s.pe.count)], inc=s.dld, amt=16)
        for half in range(2):
            bank, fr, bi = big_rot.next()
            tk = vp.on("tensor",
                       lambda h, half=half, bank=bank: h.matmul(
                           bank[:, :], ones_row[0:1, :],
                           wstage[0:1, half * 512:(half + 1) * 512],
                           start=True, stop=True),
                       waits=[(s.dld, tok_bo), fr], inc=s.pe)
            te = vp.on("vector",
                       lambda h, half=half, bank=bank: h.tensor_copy(
                           bo_bc[:, half * 512:(half + 1) * 512], bank[:, :]),
                       waits=[(s.pe, tk)], inc=s.dve)
            big_rot.set_free(bi, s.dve, te)

        vp.barrier(s.bar)

    xc = sb("a_xc", [128, 4 * 1024], F32)
    xc2 = sb("a_xc2", [128, 4 * 1024], F32)
    xTc = sb("a_xTc", [128, 8 * 512], BF16)
    fq64 = sb("a_fq64", [64, H * 512], BF16)
    fk64 = sb("a_fk64", [64, H * 512], BF16)
    ev_r = sb("a_evr", [64, 512], F32)
    ev_m = sb("a_evm", [64, 512], F32)
    ev_e = sb("a_eve", [64, 512], F32)
    v_c = sb("a_vc", [128, 4 * 1024], BF16)
    vz = sb("a_vz", [128, 4 * 65], BF16)
    ktm = sb("a_ktm", [128, 4 * 64], BF16)
    AT = sb("a_AT", [128, 4 * 512], BF16)
    att = sb("a_att", [128, 4 * 1024], BF16)
    attT = sb("a_attT", [128, 8 * 512], BF16)
    sqb = sb("a_sqb", [128, D], F32)

    xr = io["x"].rearrange("(cc tt p) d -> cc p tt d", p=128, tt=4)
    s1r = io["s1"].rearrange("(cc tt p) d -> cc p tt d", p=128, tt=4)

    xc_store_tok = [0, 0]
    for cc in range(NCH):
        xc_c = xc if cc % 2 == 0 else xc2
        tok_ld = vp.on("sync",
                       lambda h, cc=cc, xc_c=xc_c: h.dma_start(
                           out=xc_c.ap().rearrange("p (tt d) -> p tt d", tt=4),
                           in_=xr[cc]),
                       waits=[(s.dst, xc_store_tok[cc % 2])],
                       inc=s.dld, amt=16)
        for tt in range(4):
            for j in range(8):
                bank, fr, bi = tr_rot.next()
                tok_tr = vp.on("tensor",
                               lambda h, tt=tt, j=j, bank=bank, xc_c=xc_c:
                               h.transpose(
                                   bank[:, :],
                                   xc_c[:, tt * D + j * 128:tt * D + (j + 1) * 128],
                                   ident32a[:, :]),
                               waits=[(s.dld, tok_ld), fr], inc=s.pe)
                tok_e = vp.on("scalar",
                              lambda h, tt=tt, j=j, bank=bank: h.activation(
                                  xTc[:, j * 512 + tt * 128:j * 512 + (tt + 1) * 128],
                                  bank[:, :], AF.Copy),
                              waits=[(s.pe, tok_tr)], inc=s.act)
                tr_rot.set_free(bi, s.act, tok_e)
        tok_xT = s.act.count

        for (wT, bcol, nbcol, fdst) in ((wqT, bqcol, nbqcol, fq64),
                                        (wkT, bkcol, nbkcol, fk64)):
            for h_ in range(16):
                bank, fr, bi = big_rot.next()
                for kc in range(8):
                    vp.on("tensor",
                          lambda h, h_=h_, kc=kc, bank=bank, wT=wT: h.matmul(
                              bank[0:64, :],
                              wT[:, kc * 1024 + h_ * 64:kc * 1024 + (h_ + 1) * 64],
                              xTc[:, kc * 512:(kc + 1) * 512],
                              start=(kc == 0), stop=(kc == 7)),
                          waits=([(s.act, tok_xT), fr] if kc == 0 else ()),
                          inc=s.pe if kc == 7 else None)
                tok_p = s.pe.count
                vp.on("scalar",
                      lambda h, h_=h_, bank=bank, bcol=bcol: h.activation(
                          ev_r[0:64, :], bank[0:64, :], AF.Relu,
                          bias=bcol[:, h_:h_ + 1]),
                      waits=[(s.pe, tok_p), (s.dve, s.dve.count)])
                tok_m = vp.on("scalar",
                              lambda h, h_=h_, bank=bank, nbcol=nbcol: h.activation(
                                  ev_m[0:64, :], bank[0:64, :], AF.Relu,
                                  bias=nbcol[:, h_:h_ + 1], scale=-1.0),
                              inc=s.act, sync=True)
                big_rot.set_free(bi, s.act, tok_m)
                tok_x = vp.on("scalar",
                              lambda h: h.activation(ev_e[0:64, :], ev_m[0:64, :],
                                                     AF.Exp, scale=-1.0),
                              inc=s.act)
                vp.on("vector",
                      lambda h, h_=h_, fdst=fdst: h.tensor_tensor(
                          fdst[:, h_ * 512:(h_ + 1) * 512], ev_r[0:64, :], ev_e[0:64, :],
                          op=ALU.add),
                      waits=[(s.act, tok_x)], inc=s.dve)
        tok_fqk = s.dve.count

        for tt in range(4):
            for nh in range(2):
                bank, fr, bi = big_rot.next()
                for kc in range(8):
                    vp.on("tensor",
                          lambda h, tt=tt, nh=nh, kc=kc, bank=bank: h.matmul(
                              bank[:, :],
                              xTc[:, kc * 512 + tt * 128:kc * 512 + (tt + 1) * 128],
                              wvT[:, kc * 1024 + nh * 512:kc * 1024 + (nh + 1) * 512],
                              start=(kc == 0), stop=(kc == 7)),
                          waits=[fr] if kc == 0 else (),
                          inc=s.pe if kc == 7 else None)
                tok_p = s.pe.count
                tok_ev = vp.on("vector",
                               lambda h, tt=tt, nh=nh, bank=bank: h.tensor_tensor(
                                   v_c[:, tt * 1024 + nh * 512:tt * 1024 + (nh + 1) * 512],
                                   bank[:, :], bv_bc[:, nh * 512:(nh + 1) * 512],
                                   op=ALU.add),
                               waits=[(s.pe, tok_p)], inc=s.dve)
                big_rot.set_free(bi, s.dve, tok_ev)
        tok_v = s.dve.count

        szb_wait = getattr(s, "tok_szb_prev", 0)
        first_nd = True
        for h_ in range(H):
            for tt in range(4):
                tok_vz = vp.on("gpsimd",
                               lambda h, tt=tt, h_=h_: h.tensor_copy(
                                   vz[:, tt * 65:tt * 65 + 64],
                                   v_c[:, tt * 1024 + h_ * 64:tt * 1024 + (h_ + 1) * 64]),
                               waits=([(s.dve, tok_v), (s.pe, s.pe.count)]
                                      if tt == 0 else ()),
                               inc=s.pool if tt == 3 else None)
            for cki in range(4):
                bank, fr, bi = big_rot.next()
                tok_at = vp.on("tensor",
                               lambda h, h_=h_, cki=cki, bank=bank: h.matmul(
                                   bank[:, :],
                                   fk64[:, h_ * 512 + cki * 128:h_ * 512 + (cki + 1) * 128],
                                   fq64[:, h_ * 512:(h_ + 1) * 512],
                                   start=True, stop=True),
                               waits=[(s.dve, tok_fqk), fr], inc=s.pe)
                tok_ae = vp.on("vector",
                               lambda h, cki=cki, bank=bank: h.tensor_tensor(
                                   AT[:, cki * 512:(cki + 1) * 512], bank[:, :],
                                   masks[:, cki * 512:(cki + 1) * 512], op=ALU.mult),
                               waits=[(s.pe, tok_at)], inc=s.dve)
                big_rot.set_free(bi, s.dve, tok_ae)
            tok_AT = s.dve.count
            for tt in range(4):
                bank, fr, bi = tr_rot.next()
                bb = bank.ap().bitcast(BF16)[:, 0:64]
                tok_tr = vp.on("tensor",
                               lambda h, h_=h_, tt=tt, bb=bb: h.transpose(
                                   bb,
                                   fk64[:, h_ * 512 + tt * 128:h_ * 512 + (tt + 1) * 128],
                                   identb[0:64, 0:64]),
                               waits=[fr], inc=s.pe)
                tok_e = vp.on("scalar",
                              lambda h, tt=tt, bb=bb: h.activation(
                                  ktm[:, tt * 64:(tt + 1) * 64], bb, AF.Copy),
                              waits=[(s.pe, tok_tr)], inc=s.act)
                tr_rot.set_free(bi, s.act, tok_e)
            tok_ktm = s.act.count
            for tt in range(4):
                bank, fr, bi = nd_rot.next()
                vp.on("tensor",
                      lambda h, h_=h_, tt=tt, bank=bank: h.matmul(
                          bank[:, :],
                          fq64[:, h_ * 512 + tt * 128:h_ * 512 + (tt + 1) * 128],
                          Szb[:, h_ * 65:(h_ + 1) * 65],
                          start=True, stop=False),
                      waits=[(s.pool, tok_vz), fr] +
                            ([(s.dve, szb_wait)] if first_nd else []))
                first_nd = False
                for cki in range(tt + 1):
                    vp.on("tensor",
                          lambda h, tt=tt, cki=cki, bank=bank: h.matmul(
                              bank[:, :],
                              AT[:, cki * 512 + tt * 128:cki * 512 + (tt + 1) * 128],
                              vz[:, cki * 65:(cki + 1) * 65],
                              start=False, stop=(cki == tt)),
                          waits=[(s.dve, tok_AT)] if cki == 0 else (),
                          inc=s.pe if cki == tt else None)
                tok_nd = s.pe.count
                vp.on("vector",
                      lambda h, bank=bank: h.tensor_scalar(
                          s.col0[:, 0:1], bank[:, 64:65], 1e-6, None, op0=ALU.max),
                      waits=[(s.pe, tok_nd), (s.act, s.act.count)], sync=True)
                vp.on("vector",
                      lambda h: h.reciprocal(s.col1[:, 0:1], s.col0[:, 0:1]),
                      sync=True)
                tok_o = vp.on("vector",
                              lambda h, h_=h_, tt=tt, bank=bank: h.tensor_scalar(
                                  att[:, tt * 1024 + h_ * 64:tt * 1024 + (h_ + 1) * 64],
                                  bank[:, 0:64], s.col1[:, 0:1], None, op0=ALU.mult),
                              inc=s.dve)
                nd_rot.set_free(bi, s.dve, tok_o)
            for tt in range(4):
                vp.on("tensor",
                      lambda h, tt=tt: h.matmul(
                          ps_st[:, :], ktm[:, tt * 64:(tt + 1) * 64],
                          vz[:, tt * 65:(tt + 1) * 65],
                          start=(tt == 0), stop=(tt == 3)),
                      waits=([(s.act, tok_ktm), (s.dve, getattr(s, "tok_stfree", 0))]
                             if tt == 0 else ()),
                      inc=s.pe if tt == 3 else None)
            tok_stm = s.pe.count
            vp.on("vector",
                  lambda h, h_=h_: h.tensor_tensor(
                      Sz[:, h_ * 65:(h_ + 1) * 65], ps_st[:, :],
                      Sz[:, h_ * 65:(h_ + 1) * 65], op=ALU.add),
                  waits=[(s.pe, tok_stm)], sync=True)
            s.tok_stfree = vp.on("vector",
                                 lambda h, h_=h_: h.tensor_copy(
                                     Szb[:, h_ * 65:(h_ + 1) * 65],
                                     Sz[:, h_ * 65:(h_ + 1) * 65]),
                                 inc=s.dve)
        s.tok_szb_prev = s.dve.count

        for tt in range(4):
            for j in range(8):
                bank, fr, bi = tr_rot.next()
                bb = bank.ap().bitcast(BF16)[:, 0:128]
                tok_tr = vp.on("tensor",
                               lambda h, tt=tt, j=j, bb=bb: h.transpose(
                                   bb,
                                   att[:, tt * D + j * 128:tt * D + (j + 1) * 128],
                                   identb[:, :]),
                               waits=[(s.dve, s.dve.count), fr],
                               inc=s.pe)
                tok_e = vp.on("scalar",
                              lambda h, tt=tt, j=j, bb=bb: h.activation(
                                  attT[:, j * 512 + tt * 128:j * 512 + (tt + 1) * 128],
                                  bb, AF.Copy),
                              waits=[(s.pe, tok_tr)], inc=s.act)
                tr_rot.set_free(bi, s.act, tok_e)
        tok_attT = s.act.count

        for tt in range(4):
            for nh in range(2):
                bank, fr, bi = big_rot.next()
                for kc in range(8):
                    vp.on("tensor",
                          lambda h, tt=tt, nh=nh, kc=kc, bank=bank: h.matmul(
                              bank[:, :],
                              attT[:, kc * 512 + tt * 128:kc * 512 + (tt + 1) * 128],
                              woT[:, kc * 1024 + nh * 512:kc * 1024 + (nh + 1) * 512],
                              start=(kc == 0), stop=(kc == 7)),
                          waits=([(s.act, tok_attT), fr] if kc == 0 else ()),
                          inc=s.pe if kc == 7 else None)
                tok_p = s.pe.count
                vp.on("vector",
                      lambda h, nh=nh, bank=bank: h.tensor_tensor(
                          sqb[:, 0:512], bank[:, :], bo_bc[:, nh * 512:(nh + 1) * 512],
                          op=ALU.add),
                      waits=[(s.pe, tok_p)], sync=True)
                tok_res = vp.on("vector",
                                lambda h, tt=tt, nh=nh, xc_c=xc_c: h.tensor_tensor(
                                    xc_c[:, tt * D + nh * 512:tt * D + (nh + 1) * 512],
                                    sqb[:, 0:512],
                                    xc_c[:, tt * D + nh * 512:tt * D + (nh + 1) * 512],
                                    op=ALU.add),
                                inc=s.dve)
                big_rot.set_free(bi, s.dve, tok_res)
        for tt in range(4):
            tok_ln = layernorm_inplace(vp, s, xc_c[:, tt * D:(tt + 1) * D], sqb)
        tok_st1 = vp.on("gpsimd",
                        lambda h, cc=cc, xc_c=xc_c: h.dma_start(
                            out=s1r[cc],
                            in_=xc_c.ap().rearrange("p (tt d) -> p tt d", tt=4)),
                        waits=[(s.dve, tok_ln)], inc=s.dst, amt=16)
        xc_store_tok[cc % 2] = tok_st1

    vp.wait("gpsimd", s.dst, s.dst.count)
    return stk




def build_phase_c(nc, vp, io):
    s = mk_sems(vp, "c")
    H, DH = 8, 128
    SCALE = 1.0 / math.sqrt(DH)

    attnT_hbm = nc.dram_tensor("c_attnT", [1024, T], BF16, kind="Internal").ap()

    stk = ExitStack()
    sb = lambda name, shape, dt: stk.enter_context(nc.sbuf_tensor(name, shape, dt))
    psb = lambda name, shape, dt=F32: stk.enter_context(nc.psum_tensor(name, shape, dt))

    xT = sb("c_xT", [128, 8 * T], BF16)
    identb = sb("c_identb", [128, 128], BF16)
    amask = sb("c_amask", [128, 128], F32)
    hs_sb = sb("c_hs", [1, 8], F32)
    dirs_b = sb("c_dirs", [3, 8], BF16)
    dirs_f = sb("c_dirsf", [3, 8], F32)
    s.col0 = sb("c_col0", [128, 1], F32)
    s.col1 = sb("c_col1", [128, 1], F32)
    s.col2 = sb("c_col2", [128, 1], F32)

    ps_tr1 = psb("c_pstr1", [128, 512], BF16)
    ps_tr2 = psb("c_pstr2", [128, 512], BF16)
    ps_b1 = psb("c_psb1", [128, 512])
    ps_b2 = psb("c_psb2", [128, 512])
    ps_b3 = psb("c_psb3", [128, 512])
    ps_av = psb("c_psav", [128, 512])
    tr_rot = PsRot([ps_tr1, ps_tr2])
    big_rot = PsRot([ps_b1, ps_b2, ps_b3])

    make_identity(vp, identb)
    vp.on("gpsimd", lambda h: h.memset(amask[:, :], 0.0))
    vp.on("gpsimd",
          lambda h: h.affine_select(
              out=amask[:, :], in_=amask[:, :], compare_op=ALU.is_ge, fill=-60000.0,
              base=0, pattern=[[-1, 128]], channel_multiplier=1),
          inc=s.pool)
    vp.wait("tensor", s.pool, 1)

    tok = vp.on("sync", lambda h: h.dma_start(
        out=hs_sb[:, :], in_=io["head_scales"].rearrange("(a e) -> a e", a=1)),
        inc=s.dld, amt=16)
    for h_ in range(8):
        tok = vp.on("sync",
                    lambda hh, h_=h_: hh.dma_start(
                        out=dirs_f[:, h_:h_ + 1],
                        in_=io["head_dirs"][h_].rearrange("(a b) -> a b", b=1)),
                    inc=s.dld, amt=16)
    vp.on("vector", lambda h: h.tensor_copy(dirs_b[:, :], dirs_f[:, :]),
          waits=[(s.dld, s.dld.count)], inc=s.dve)

    s2r = io["s2"].rearrange("(tb tt p) d -> tb p tt d", p=128, tt=4)
    s3r = io["s3"].rearrange("(tb tt p) d -> tb p tt d", p=128, tt=4)
    with nc.sbuf_tensor("c_sc0", [128, 4 * D], F32) as sc0, \
         nc.sbuf_tensor("c_scb0", [128, 4 * D], BF16) as scb0:
        for tb in range(8):
            tok_ld = vp.on("sync",
                           lambda h, tb=tb: h.dma_start(
                               out=sc0.ap().rearrange("p (tt d) -> p tt d", tt=4),
                               in_=s2r[tb]),
                           waits=[(s.dve, s.dve.count)],
                           inc=s.dld, amt=16)
            tok_cv = vp.on("vector", lambda h: h.tensor_copy(scb0[:, :], sc0[:, :]),
                           waits=[(s.dld, tok_ld), (s.pe, s.pe.count)], inc=s.dve)
            for tt in range(4):
                for j in range(8):
                    bank, fr, bi = tr_rot.next()
                    tok_tr = vp.on("tensor",
                                   lambda h, tt=tt, j=j, bank=bank: h.transpose(
                                       bank[:, 0:128],
                                       scb0[:, tt * D + j * 128:tt * D + (j + 1) * 128],
                                       identb[:, :]),
                                   waits=[(s.dve, tok_cv), fr], inc=s.pe)
                    tok_e = vp.on("vector",
                                  lambda h, tb=tb, tt=tt, j=j, bank=bank: h.tensor_copy(
                                      xT[:, j * T + tb * 512 + tt * 128:
                                         j * T + tb * 512 + (tt + 1) * 128],
                                      bank[:, 0:128]),
                                  waits=[(s.pe, tok_tr)], inc=s.dve)
                    tr_rot.set_free(bi, s.dve, tok_e)
        vp.barrier(s.bar)

    with nc.sbuf_tensor("c_whs", [128, 8 * 384], F32) as whs, \
         nc.sbuf_tensor("c_whb", [128, 8 * 384], BF16) as whb, \
         nc.sbuf_tensor("c_whb2", [128, 8 * 384], BF16) as whb2, \
         nc.sbuf_tensor("c_QT", [128, T], BF16) as QT, \
         nc.sbuf_tensor("c_KT", [128, T], BF16) as KT, \
         nc.sbuf_tensor("c_Vtm", [128, 32 * 128], BF16) as Vtm, \
         nc.sbuf_tensor("c_dcols", [128, 8], F32) as dcols, \
         nc.sbuf_tensor("c_A4a", [128, 4 * T], BF16) as A4a, \
         nc.sbuf_tensor("c_A4b", [128, 4 * T], BF16) as A4b, \
         nc.sbuf_tensor("c_ATg1", [128, 512], BF16) as ATg1, \
         nc.sbuf_tensor("c_ATg2", [128, 512], BF16) as ATg2, \
         nc.sbuf_tensor("c_avb", [128, 512], BF16) as avb, \
         nc.sbuf_tensor("c_avb2", [128, 512], BF16) as avb2, \
         nc.sbuf_tensor("c_qp", [1, T], BF16) as qp_row, \
         nc.sbuf_tensor("c_kpa", [1, T], BF16) as kpa_row:
        wh_pe_done = [0, 0]
        wh_cv_tok = [0, 0]
        s.a4_free = [0, 0]
        s.avb_store = [0, 0]

        wh_cv_any = [0]

        def emit_wh_load(h_):
            par = h_ % 2
            whs_t = whs
            whb_t = whb if par == 0 else whb2
            tok_wld = 0
            for kc in range(8):
                for jj, off in enumerate((0, 1024, 2048)):
                    tok_wld = vp.on("sync",
                                    lambda h, kc=kc, jj=jj, off=off, h_=h_,
                                    whs_t=whs_t: h.dma_start(
                                        out=whs_t[:, kc * 384 + jj * 128:
                                                  kc * 384 + (jj + 1) * 128],
                                        in_=io["Wqkv"][kc * 128:(kc + 1) * 128,
                                                       off + h_ * 128:off + (h_ + 1) * 128]),
                                    waits=([(s.dve, wh_cv_any[0])]
                                           if (kc == 0 and jj == 0) else ()),
                                    inc=s.dld, amt=16)
            wh_cv_tok[par] = vp.on("vector",
                                   lambda h, whs_t=whs_t, whb_t=whb_t: h.tensor_copy(
                                       whb_t[:, :], whs_t[:, :]),
                                   waits=[(s.dld, tok_wld), (s.pe, wh_pe_done[par])],
                                   inc=s.dve)
            wh_cv_any[0] = wh_cv_tok[par]

        emit_wh_load(0)
        for h_ in range(H):
            whb_t = whb if h_ % 2 == 0 else whb2
            tok_wcv = wh_cv_tok[h_ % 2]
            if h_ + 1 < H:
                emit_wh_load(h_ + 1)

            for dst, jj, scale in ((QT, 0, SCALE), (KT, 1, None)):
                for g in range(8):
                    bank, fr, bi = big_rot.next()
                    for kc in range(8):
                        vp.on("tensor",
                              lambda h, dst=dst, jj=jj, g=g, kc=kc, bank=bank,
                              whb_t=whb_t: h.matmul(
                                  bank[:, :],
                                  whb_t[:, kc * 384 + jj * 128:kc * 384 + (jj + 1) * 128],
                                  xT[:, kc * T + g * 512:kc * T + (g + 1) * 512],
                                  start=(kc == 0), stop=(kc == 7)),
                              waits=[(s.dve, tok_wcv), fr] if kc == 0 else (),
                              inc=s.pe if kc == 7 else None)
                    tok_p = s.pe.count
                    if scale is None:
                        tok_e = vp.on("vector",
                                      lambda h, dst=dst, g=g, bank=bank: h.tensor_copy(
                                          dst[:, g * 512:(g + 1) * 512], bank[:, :]),
                                      waits=[(s.pe, tok_p)], inc=s.dve)
                    else:
                        tok_e = vp.on("vector",
                                      lambda h, dst=dst, g=g, bank=bank, scale=scale:
                                      h.tensor_scalar(dst[:, g * 512:(g + 1) * 512],
                                                      bank[:, :], scale, None,
                                                      op0=ALU.mult),
                                      waits=[(s.pe, tok_p)], inc=s.dve)
                    big_rot.set_free(bi, s.dve, tok_e)
            for g in range(8):
                bank, fr, bi = big_rot.next()
                for kc in range(8):
                    vp.on("tensor",
                          lambda h, g=g, kc=kc, bank=bank, whb_t=whb_t: h.matmul(
                              bank[:, :],
                              whb_t[:, kc * 384 + 2 * 128:kc * 384 + 3 * 128],
                              xT[:, kc * T + g * 512:kc * T + (g + 1) * 512],
                              start=(kc == 0), stop=(kc == 7)),
                          waits=[fr] if kc == 0 else (),
                          inc=s.pe if kc == 7 else None)
                tok_p = s.pe.count
                tok_e = vp.on("vector",
                              lambda h, g=g, bank=bank: h.tensor_copy(
                                  A4a[:, g * 512:(g + 1) * 512], bank[:, :]),
                              waits=[(s.pe, tok_p)], inc=s.dve)
                big_rot.set_free(bi, s.dve, tok_e)
            tok_vt = s.dve.count
            for tt in range(32):
                bank, fr, bi = tr_rot.next()
                tok_tr = vp.on("tensor",
                               lambda h, tt=tt, bank=bank: h.transpose(
                                   bank[:, 0:128], A4a[:, tt * 128:(tt + 1) * 128],
                                   identb[:, :]),
                               waits=[(s.dve, tok_vt), fr], inc=s.pe)
                tok_e = vp.on("vector",
                              lambda h, tt=tt, bank=bank: h.tensor_copy(
                                  Vtm[:, tt * 128:(tt + 1) * 128], bank[:, 0:128]),
                              waits=[(s.pe, tok_tr)], inc=s.dve)
                tr_rot.set_free(bi, s.dve, tok_e)

            wh_pe_done[h_ % 2] = s.pe.count
            for g in range(8):
                bank, fr, bi = big_rot.next()
                tk = vp.on("tensor",
                           lambda h, g=g, h_=h_, bank=bank: h.matmul(
                               bank[0:1, :], dirs_b[:, h_:h_ + 1],
                               QT[0:3, g * 512:(g + 1) * 512], start=True, stop=True),
                           waits=[(s.dve, s.dve.count), fr], inc=s.pe)
                te = vp.on("vector",
                           lambda h, g=g, bank=bank: h.tensor_copy(
                               qp_row[:, g * 512:(g + 1) * 512], bank[0:1, :]),
                           waits=[(s.pe, tk)], inc=s.dve)
                big_rot.set_free(bi, s.dve, te)
            for g in range(8):
                bank, fr, bi = big_rot.next()
                tk = vp.on("tensor",
                           lambda h, g=g, h_=h_, bank=bank: h.matmul(
                               bank[0:1, :], dirs_b[:, h_:h_ + 1],
                               KT[0:3, g * 512:(g + 1) * 512], start=True, stop=True),
                           waits=[fr], inc=s.pe)
                te = vp.on("vector",
                           lambda h, g=g, h_=h_, bank=bank: h.tensor_scalar(
                               kpa_row[:, g * 512:(g + 1) * 512], bank[0:1, :],
                               hs_sb[0:1, h_:h_ + 1], 1.0 / SCALE,
                               op0=ALU.mult, op1=ALU.mult),
                           waits=[(s.pe, tk)], inc=s.dve)
                big_rot.set_free(bi, s.dve, te)
            tok_rows = s.dve.count

            tokA = {}

            def emit_S(qg):
                A4 = A4a if qg % 2 == 0 else A4b
                Wg = (qg + 1) * 512
                for li in range(4):
                    i = qg * 4 + li
                    W = (i + 1) * 128
                    ng = (W + 511) // 512
                    for g in range(ng):
                        w = min(512, W - g * 512)
                        bank, fr, bi = big_rot.next()
                        vp.on("tensor",
                              lambda h, i=i, g=g, w=w, bank=bank: h.matmul(
                                  bank[:, 0:w], QT[:, i * 128:(i + 1) * 128],
                                  KT[:, g * 512:g * 512 + w],
                                  start=True, stop=False),
                              waits=[(s.dve, tok_rows), fr])
                        tok_sm = vp.on("tensor",
                                       lambda h, i=i, g=g, w=w, bank=bank: h.matmul(
                                           bank[:, 0:w],
                                           qp_row[:, i * 128:(i + 1) * 128],
                                           kpa_row[:, g * 512:g * 512 + w],
                                           start=False, stop=True),
                                       inc=s.pe)
                        exp_waits = [(s.pe, tok_sm)]
                        if g == 0:
                            exp_waits.append((s.dve, s.dve.count))
                            exp_waits.append((s.pe, s.a4_free[qg % 2]))
                        if g == ng - 1:
                            tok_msk = vp.on("vector",
                                            lambda h, w=w, bank=bank: h.tensor_tensor(
                                                bank[:, w - 128:w], bank[:, w - 128:w],
                                                amask[:, :], op=ALU.add),
                                            waits=[(s.pe, tok_sm)], inc=s.dve)
                            exp_waits.append((s.dve, tok_msk))
                        tok_exp = vp.on("scalar",
                                        lambda h, li=li, g=g, w=w, bank=bank, A4=A4:
                                        h.activation(
                                            A4[:, li * T + g * 512:li * T + g * 512 + w],
                                            bank[:, 0:w], AF.Exp,
                                            accum_out=dcols[:, g:g + 1]),
                                        waits=exp_waits, inc=s.act)
                        big_rot.set_free(bi, s.act, tok_exp)
                    vp.on("vector",
                          lambda h, ng=ng: h.reduce_sum(s.col1[:, 0:1],
                                                        dcols[:, 0:ng], axis=AX.X),
                          waits=[(s.act, s.act.count)], sync=True)
                    vp.on("vector", lambda h: h.reciprocal(s.col2[:, 0:1],
                                                           s.col1[:, 0:1]),
                          sync=True)
                    tok_nr = vp.on("vector",
                                   lambda h, li=li, W=W, A4=A4: h.tensor_scalar(
                                       A4[:, li * T:li * T + W], A4[:, li * T:li * T + W],
                                       s.col2[:, 0:1], None, op0=ALU.mult),
                                   inc=s.dve)
                    if W < Wg:
                        vp.on("gpsimd",
                              lambda h, li=li, W=W, Wg=Wg, A4=A4: h.memset(
                                  A4[:, li * T + W:li * T + Wg], 0.0),
                              waits=[(s.dve, tok_nr), (s.pe, s.pe.count)], inc=s.pool)
                tokA[qg] = (s.dve.count, s.pool.count)

            def emit_AV(qg, h_):
                A4 = A4a if qg % 2 == 0 else A4b
                tok_A = tokA[qg]
                atg = [ATg1, ATg2]
                atg_free = [None, None]
                nkt = (qg + 1) * 4
                pend = None

                def emit_av_mm(kt, ab, tok_ae):
                    tok_av = vp.on("tensor",
                                   lambda h, kt=kt, ab=ab: h.matmul(
                                       ps_av[:, :], Vtm[:, kt * 128:(kt + 1) * 128],
                                       ab[:, :], start=(kt == 0),
                                       stop=(kt == nkt - 1)),
                                   waits=[(s.dve, tok_ae)] +
                                         ([(s.dve, getattr(s, "tok_avfree", 0))]
                                          if kt == 0 else []),
                                   inc=s.pe)
                    atg_free[kt % 2] = (s.pe, tok_av)

                for kt in range(nkt):
                    bank, fr, bi = tr_rot.next()
                    for li in range(4):
                        vp.on("tensor",
                              lambda h, li=li, kt=kt, bank=bank, A4=A4: h.transpose(
                                  bank[:, li * 128:(li + 1) * 128],
                                  A4[:, li * T + kt * 128:li * T + (kt + 1) * 128],
                                  identb[:, :]),
                              waits=([(s.dve, tok_A[0]), (s.pool, tok_A[1]), fr]
                                     if li == 0 else ()),
                              inc=s.pe if li == 3 else None)
                    tok_tr = s.pe.count
                    if pend is not None:
                        emit_av_mm(*pend)
                    ab = atg[kt % 2]
                    tok_ae = vp.on("vector",
                                   lambda h, ab=ab, bank=bank: h.tensor_copy(
                                       ab[:, :], bank[:, :]),
                                   waits=[(s.pe, tok_tr)] +
                                         ([atg_free[kt % 2]] if atg_free[kt % 2] else []),
                                   inc=s.dve)
                    tr_rot.set_free(bi, s.dve, tok_ae)
                    pend = (kt, ab, tok_ae)
                emit_av_mm(*pend)
                avb_t = avb if qg % 2 == 0 else avb2
                tok_ave = vp.on("vector",
                                lambda h, avb_t=avb_t: h.tensor_copy(avb_t[:, :],
                                                                     ps_av[:, :]),
                                waits=[(s.pe, s.pe.count),
                                       (s.dst, s.avb_store[qg % 2])],
                                inc=s.dve)
                s.tok_avfree = tok_ave
                s.avb_store[qg % 2] = vp.on(
                    "gpsimd",
                    lambda h, h_=h_, qg=qg, avb_t=avb_t: h.dma_start(
                        out=attnT_hbm[h_ * 128:(h_ + 1) * 128,
                                      qg * 512:(qg + 1) * 512],
                        in_=avb_t[:, :]),
                    waits=[(s.dve, tok_ave)], inc=s.dst, amt=16)
                s.a4_free[qg % 2] = s.pe.count

            prev = None
            for qg in range(8):
                emit_S(qg)
                if prev is not None:
                    emit_AV(prev, h_)
                prev = qg
            emit_AV(prev, h_)

        vp.barrier(s.bar)

    with nc.sbuf_tensor("c_wout", [128, 8 * 1024], BF16) as wout, \
         nc.sbuf_tensor("c_wstage", [128, 1024], F32) as wstage, \
         nc.sbuf_tensor("c_aT", [128, 8 * 512], BF16) as aT, \
         nc.sbuf_tensor("c_aT2", [128, 8 * 512], BF16) as aT2, \
         nc.sbuf_tensor("c_sc", [128, 4 * D], F32) as sc, \
         nc.sbuf_tensor("c_sc2", [128, 4 * D], F32) as sc2, \
         nc.sbuf_tensor("c_sqb", [128, D], F32) as sqb:
        for kc in range(8):
            tok_ld = vp.on("sync",
                           lambda h, kc=kc: h.dma_start(
                               out=wstage[:, :],
                               in_=io["Wout"][kc * 128:(kc + 1) * 128, :]),
                           waits=[(s.dve, s.dve.count)],
                           inc=s.dld, amt=16)
            vp.on("vector",
                  lambda h, kc=kc: h.tensor_copy(wout[:, kc * 1024:(kc + 1) * 1024],
                                                 wstage[:, :]),
                  waits=[(s.dld, tok_ld)], inc=s.dve, sync=True)
        aTr = attnT_hbm.rearrange("(kc p) t -> p kc t", p=128)
        ld_toks = [(0, 0), (0, 0)]
        st_toks = [0, 0]
        pe_done = [0, 0]

        def emit_loads(tb):
            par = tb % 2
            aT_t = aT if par == 0 else aT2
            sc_t = sc if par == 0 else sc2
            ta = vp.on("sync",
                       lambda h, tb=tb, aT_t=aT_t: h.dma_start(
                           out=aT_t.ap().rearrange("p (kc t) -> p kc t", kc=8),
                           in_=aTr[:, :, tb * 512:(tb + 1) * 512]),
                       waits=[(s.pe, pe_done[par])],
                       inc=s.dld, amt=16)
            ts_ = vp.on("sync",
                        lambda h, tb=tb, sc_t=sc_t: h.dma_start(
                            out=sc_t.ap().rearrange("p (tt d) -> p tt d", tt=4),
                            in_=s2r[tb]),
                        waits=[(s.dst, st_toks[par])],
                        inc=s.dld, amt=16)
            ld_toks[par] = (ta, ts_)

        emit_loads(0)
        for tb in range(8):
            par = tb % 2
            aT_t = aT if par == 0 else aT2
            sc_t = sc if par == 0 else sc2
            tok_lda, tok_lds = ld_toks[par]
            if tb + 1 < 8:
                emit_loads(tb + 1)
            for tt in range(4):
                for nh in range(2):
                    bank, fr, bi = big_rot.next()
                    for kc in range(8):
                        vp.on("tensor",
                              lambda h, tt=tt, nh=nh, kc=kc, bank=bank, aT_t=aT_t:
                              h.matmul(
                                  bank[:, :],
                                  aT_t[:, kc * 512 + tt * 128:kc * 512 + (tt + 1) * 128],
                                  wout[:, kc * 1024 + nh * 512:kc * 1024 + (nh + 1) * 512],
                                  start=(kc == 0), stop=(kc == 7)),
                              waits=[(s.dld, tok_lda), fr] if kc == 0 else (),
                              inc=s.pe if kc == 7 else None)
                    tok_p = s.pe.count
                    tok_res = vp.on("vector",
                                    lambda h, tt=tt, nh=nh, bank=bank, sc_t=sc_t:
                                    h.tensor_tensor(
                                        sc_t[:, tt * D + nh * 512:tt * D + (nh + 1) * 512],
                                        bank[:, :],
                                        sc_t[:, tt * D + nh * 512:tt * D + (nh + 1) * 512],
                                        op=ALU.add),
                                    waits=[(s.pe, tok_p), (s.dld, tok_lds)],
                                    inc=s.dve)
                    big_rot.set_free(bi, s.dve, tok_res)
            pe_done[par] = s.pe.count
            for tt in range(4):
                tok_ln = layernorm_inplace(vp, s, sc_t[:, tt * D:(tt + 1) * D], sqb)
            st_toks[par] = vp.on("gpsimd",
                                 lambda h, tb=tb, sc_t=sc_t: h.dma_start(
                                     out=s3r[tb],
                                     in_=sc_t.ap().rearrange("p (tt d) -> p tt d",
                                                             tt=4)),
                                 waits=[(s.dve, tok_ln)], inc=s.dst, amt=16)

    vp.wait("gpsimd", s.dst, s.dst.count)
    return stk




def load_convert_bg(vp, s, w_hbm, rows, cols, wstages, dst, pe_gate):
    n_rt = rows // 128
    lag = 0 if wstages[0] is wstages[1] else 1

    def chunk(rt):
        ws = wstages[rt % 2]
        tok_ld = vp.on("sync",
                       lambda h, rt=rt, ws=ws: h.dma_start(
                           out=ws[:, 0:cols],
                           in_=w_hbm[rt * 128:(rt + 1) * 128, :]),
                       waits=[(s.pool, max(0, s.pool.count - lag))],
                       inc=s.dld, amt=16)
        vp.on("gpsimd",
              lambda h, rt=rt, ws=ws: h.tensor_copy(
                  dst[:, rt * cols:(rt + 1) * cols], ws[:, 0:cols]),
              waits=[(s.dld, tok_ld), (s.pe, pe_gate())], inc=s.pool)

    return [lambda rt=rt: chunk(rt) for rt in range(n_rt)]


def load_convert(vp, s, w_hbm, rows, cols, wstage, dst, pe_gate=None):
    n_rt = rows // 128
    for rt in range(n_rt):
        tok_ld = vp.on("sync",
                       lambda h, rt=rt: h.dma_start(
                           out=wstage[:, 0:cols],
                           in_=w_hbm[rt * 128:(rt + 1) * 128, :]),
                       waits=[(s.dve, s.dve.count)],
                       inc=s.dld, amt=16)
        vp.on("vector",
              lambda h, rt=rt: h.tensor_copy(dst[:, rt * cols:(rt + 1) * cols],
                                             wstage[:, 0:cols]),
              waits=[(s.dld, tok_ld),
                     (s.pe, s.pe.count if pe_gate is None else pe_gate)],
              inc=s.dve)


def build_phase_d(nc, vp, io):
    s = mk_sems(vp, "d")
    E = 16

    acc = nc.dram_tensor("d_acc", [T, D], F32, kind="Internal").ap()

    stk = ExitStack()
    sb = lambda name, shape, dt: stk.enter_context(nc.sbuf_tensor(name, shape, dt))
    psb = lambda name, shape, dt=F32: stk.enter_context(nc.psum_tensor(name, shape, dt))

    xT = sb("d_xT", [128, 8 * T], BF16)
    comb = sb("d_comb", [128, 32 * 16], F32)
    ones_row = sb("d_ones", [1, 128], F32)
    s.col0 = sb("d_col0", [128, 1], F32)
    s.col1 = sb("d_col1", [128, 1], F32)
    s.col2 = sb("d_col2", [128, 1], F32)
    rstk = ExitStack()
    rsb = lambda name, shape, dt: rstk.enter_context(nc.sbuf_tensor(name, shape, dt))
    ident32 = rsb("d_ident32", [128, 128], F32)
    rwT32 = rsb("d_rwT", [128, 8 * 16], F32)
    bias_bc = rsb("d_biasbc", [128, 16], F32)
    scores = rsb("d_scores", [128, 32 * 16], F32)
    r_sb = rsb("d_rsb", [128, 32 * 16], F32)
    r_eq = rsb("d_req", [128, 32 * 16], F32)
    r_msk = rsb("d_rmsk", [128, 32 * 16], F32)
    r_wrk = rsb("d_rwrk", [128, 32 * 16], F32)
    r_tw = rsb("d_rtw", [128, 32 * 16], F32)
    gm1 = rsb("d_gm1", [128, 32 * 4], F32)
    gm2 = rsb("d_gm2", [128, 32 * 4], F32)
    c32a = rsb("d_c32a", [128, 32], F32)
    c32b = rsb("d_c32b", [128, 32], F32)

    ps_tr1 = psb("d_pstr1", [128, 128])
    ps_tr2 = psb("d_pstr2", [128, 128])
    ps_r = psb("d_psr", [128, 16])
    ps_a = psb("d_psa", [128, 512])
    ps_b = psb("d_psb", [128, 512])
    ps_c = psb("d_psc", [128, 512])
    ps_d = psb("d_psd", [128, 512])
    tr_rot = PsRot([ps_tr1, ps_tr2])
    hu_rot = PsRot([ps_a, ps_b])
    y_rot = PsRot([ps_c, ps_d])

    make_identity(vp, ident32)
    vp.on("gpsimd", lambda h: h.memset(ones_row[:, :], 1.0), inc=s.pool)
    vp.wait("tensor", s.pool, 1)

    s3r = io["s3"].rearrange("(tb tt p) d -> tb p tt d", p=128, tt=4)
    outr = io["out"].rearrange("(tb tt p) d -> tb p tt d", p=128, tt=4)
    accr = acc.rearrange("(g tt p) d -> g p tt d", p=128, tt=4)

    with nc.sbuf_tensor("d_rstage", [128, 1024], F32) as rstage:
        tok = vp.on("sync", lambda h: h.dma_start(out=rstage[0:16, :],
                                                  in_=io["router_w"]),
                    inc=s.dld, amt=16)
        vp.wait("tensor", s.dld, tok)
        for j in range(8):
            bank, fr, bi = tr_rot.next()
            tk = vp.on("tensor",
                       lambda h, j=j, bank=bank: h.transpose(
                           bank[:, 0:16], rstage[0:16, j * 128:(j + 1) * 128],
                           ident32[0:16, 0:16]),
                       waits=[fr], inc=s.pe)
            te = vp.on("vector",
                       lambda h, j=j, bank=bank: h.tensor_copy(
                           rwT32[:, j * 16:(j + 1) * 16], bank[:, 0:16]),
                       waits=[(s.pe, tk)], inc=s.dve)
            tr_rot.set_free(bi, s.dve, te)
        tokb = vp.on("sync", lambda h: h.dma_start(
            out=rstage[0:1, 0:16], in_=io["bias"].rearrange("(a e) -> a e", a=1)),
            waits=[(s.pe, s.pe.count)], inc=s.dld, amt=16)
        tk = vp.on("tensor", lambda h: h.matmul(ps_r[:, :], ones_row[0:1, :],
                                                rstage[0:1, 0:16], start=True,
                                                stop=True),
                   waits=[(s.dld, tokb)], inc=s.pe)
        vp.on("vector", lambda h: h.tensor_copy(bias_bc[:, :], ps_r[:, :]),
              waits=[(s.pe, tk)], inc=s.dve)
        vp.barrier(s.bar)

    with nc.sbuf_tensor("d_sc0", [128, 4 * D], F32) as sc0, \
         nc.sbuf_tensor("d_xT32", [128, 8 * 128], F32) as xT32:
        for tb in range(8):
            tok_ld = vp.on("sync",
                           lambda h, tb=tb: h.dma_start(
                               out=sc0.ap().rearrange("p (tt d) -> p tt d", tt=4),
                               in_=s3r[tb]),
                           waits=[(s.pe, s.pe.count)],
                           inc=s.dld, amt=16)
            for tt in range(4):
                for j in range(8):
                    bank, fr, bi = tr_rot.next()
                    tok_tr = vp.on("tensor",
                                   lambda h, tt=tt, j=j, bank=bank: h.transpose(
                                       bank[:, :],
                                       sc0[:, tt * D + j * 128:tt * D + (j + 1) * 128],
                                       ident32[:, :]),
                                   waits=[(s.dld, tok_ld), fr], inc=s.pe)
                    vp.on("vector",
                          lambda h, tb=tb, tt=tt, j=j, bank=bank: h.tensor_copy(
                              xT[:, j * T + tb * 512 + tt * 128:
                                 j * T + tb * 512 + (tt + 1) * 128],
                              bank[:, :]),
                          waits=[(s.pe, tok_tr)])
                    tok_e = vp.on("vector",
                                  lambda h, j=j, bank=bank: h.tensor_copy(
                                      xT32[:, j * 128:(j + 1) * 128], bank[:, :]),
                                  inc=s.dve)
                    tr_rot.set_free(bi, s.dve, tok_e)
                tok_x32 = s.dve.count
                for j in range(8):
                    vp.on("tensor",
                          lambda h, j=j: h.matmul(
                              ps_r[:, :], xT32[:, j * 128:(j + 1) * 128],
                              rwT32[:, j * 16:(j + 1) * 16],
                              start=(j == 0), stop=(j == 7)),
                          waits=([(s.dve, tok_x32), (s.act, s.act.count)]
                                 if j == 0 else ()),
                          inc=s.pe if j == 7 else None)
                vp.on("scalar",
                      lambda h, tb=tb, tt=tt: h.activation(
                          scores[:, (tb * 4 + tt) * 16:(tb * 4 + tt + 1) * 16],
                          ps_r[:, :], AF.Sigmoid),
                      waits=[(s.pe, s.pe.count)], inc=s.act)
        vp.barrier(s.bar)

    NT, NGv, EPGv = 32, 4, 4
    v3 = lambda buf: buf.ap().rearrange("p (o e) -> p o e", e=16)
    v4 = lambda buf: buf.ap().rearrange("p (o g j) -> p o g j", g=4, j=4)
    vg = lambda buf: buf.ap().rearrange("p (o g) -> p o g", g=4)
    B = [128, NT, 16]
    B4 = [128, NT, 4, 4]
    BG = [128, NT, 4]
    ro = lambda fn: vp.on("vector", fn, sync=True)
    ro(lambda h: h.tensor_tensor(
        v3(r_sb), v3(scores),
        bias_bc.ap().rearrange("p (o e) -> p o e", o=1).broadcast_to(B),
        op=ALU.add))
    ro(lambda h: h.reduce_max(vg(gm1), v4(r_sb), axis=AX.X))
    ro(lambda h: h.tensor_tensor(
        v4(r_eq), v4(r_sb),
        gm1.ap().rearrange("p (o g j) -> p o g j", g=4, j=1).broadcast_to(B4),
        op=ALU.is_ge))
    ro(lambda h: h.scalar_tensor_tensor(v4(r_wrk), v4(r_eq), -1e9, v4(r_sb),
                                        op0=ALU.mult, op1=ALU.add))
    ro(lambda h: h.reduce_max(vg(gm2), v4(r_wrk), axis=AX.X))
    ro(lambda h: h.tensor_tensor(vg(gm1), vg(gm1), vg(gm2), op=ALU.add))
    ro(lambda h: h.reduce_max(c32a.ap(), vg(gm1), axis=AX.X))
    ro(lambda h: h.tensor_tensor(
        vg(gm2), vg(gm1),
        c32a.ap().rearrange("p (o g) -> p o g", g=1).broadcast_to(BG),
        op=ALU.is_ge))
    ro(lambda h: h.scalar_tensor_tensor(vg(gm2), vg(gm2), -1e9, vg(gm1),
                                        op0=ALU.mult, op1=ALU.add))
    ro(lambda h: h.reduce_max(c32b.ap(), vg(gm2), axis=AX.X))
    ro(lambda h: h.tensor_tensor(
        vg(gm2), vg(gm1),
        c32b.ap().rearrange("p (o g) -> p o g", g=1).broadcast_to(BG),
        op=ALU.is_ge))
    ro(lambda h: h.tensor_scalar(r_wrk.ap(), r_sb.ap(), 1.0, None, op0=ALU.add))
    ro(lambda h: h.tensor_tensor(
        v4(r_msk), v4(r_wrk),
        gm2.ap().rearrange("p (o g j) -> p o g j", g=4, j=1).broadcast_to(B4),
        op=ALU.mult))
    ro(lambda h: h.tensor_scalar(r_msk.ap(), r_msk.ap(), -1.0, None, op0=ALU.add))
    ro(lambda h: h.tensor_copy(r_wrk.ap(), r_msk.ap()))
    for _ in range(3):
        ro(lambda h: h.reduce_max(c32a.ap(), v3(r_wrk), axis=AX.X))
        ro(lambda h: h.tensor_tensor(
            v3(r_eq), v3(r_wrk),
            c32a.ap().rearrange("p (o e) -> p o e", e=1).broadcast_to(B),
            op=ALU.is_ge))
        ro(lambda h: h.scalar_tensor_tensor(v3(r_wrk), v3(r_eq), -1e9, v3(r_wrk),
                                            op0=ALU.mult, op1=ALU.add))
    ro(lambda h: h.reduce_max(c32a.ap(), v3(r_wrk), axis=AX.X))
    ro(lambda h: h.tensor_tensor(
        v3(r_eq), v3(r_msk),
        c32a.ap().rearrange("p (o e) -> p o e", e=1).broadcast_to(B),
        op=ALU.is_ge))
    ro(lambda h: h.tensor_tensor(r_tw.ap(), scores.ap(), r_eq.ap(), op=ALU.mult))
    ro(lambda h: h.reduce_sum(c32a.ap(), v3(r_tw), axis=AX.X))
    ro(lambda h: h.tensor_scalar(c32a.ap(), c32a.ap(), 1e-20, None, op0=ALU.add))
    ro(lambda h: h.reciprocal(c32b.ap(), c32a.ap()))
    tok_comb = vp.on("vector",
                     lambda h: h.scalar_tensor_tensor(
                         v3(comb), v3(r_tw), 2.5,
                         c32b.ap().rearrange("p (o e) -> p o e", e=1).broadcast_to(B),
                         op0=ALU.mult, op1=ALU.mult),
                     inc=s.dve, sync=True)

    if "dbg_scores" in io:
        vp.on("gpsimd", lambda h: h.dma_start(out=io["dbg_scores"], in_=scores[:, :]),
              waits=[(s.dve, s.dve.count), (s.act, s.act.count)], inc=s.dst, amt=16)
    if "dbg_comb" in io:
        vp.on("gpsimd", lambda h: h.dma_start(out=io["dbg_comb"], in_=comb[:, :]),
              waits=[(s.dve, s.dve.count)], inc=s.dst, amt=16)
    vp.barrier(s.bar)
    rstk.close()

    tok_accg = [0] * 8

    ysb_free = [0, 0]

    def expert_ffn(gT, uT, dT, n_ft, f_stride, comb_e, p_buf, silb, ysb_pair,
                   tok_w, accum, w_sem=None, bg_step=None):
        two_ysb = ysb_pair[0] is not ysb_pair[1]
        for g in range(8):
            bslot = (g % 2) if two_ysb else 0
            for ft in range(n_ft):
                bank_h, fr_h, bi_h = hu_rot.next()
                for kc in range(8):
                    vp.on("tensor",
                          lambda h, g=g, ft=ft, kc=kc, bank=bank_h, gT=gT: h.matmul(
                              bank[:, :],
                              gT[:, kc * f_stride + ft * 128:kc * f_stride + (ft + 1) * 128],
                              xT[:, kc * T + g * 512:kc * T + (g + 1) * 512],
                              start=(kc == 0), stop=(kc == 7)),
                          waits=[((w_sem or s.dve), tok_w), fr_h]
                          if kc == 0 else (),
                          inc=s.pe if kc == 7 else None)
                tok_h = s.pe.count
                bank_u, fr_u, bi_u = hu_rot.next()
                for kc in range(8):
                    vp.on("tensor",
                          lambda h, g=g, ft=ft, kc=kc, bank=bank_u, uT=uT: h.matmul(
                              bank[:, :],
                              uT[:, kc * f_stride + ft * 128:kc * f_stride + (ft + 1) * 128],
                              xT[:, kc * T + g * 512:kc * T + (g + 1) * 512],
                              start=(kc == 0), stop=(kc == 7)),
                          waits=[fr_u] if kc == 0 else (),
                          inc=s.pe if kc == 7 else None)
                tok_u = s.pe.count
                tok_sil = vp.on("scalar",
                                lambda h, bank=bank_h: h.activation(
                                    silb[:, :], bank[:, :], AF.Silu),
                                waits=[(s.pe, tok_h), (s.dve, s.dve.count)],
                                inc=s.act)
                hu_rot.set_free(bi_h, s.act, tok_sil)
                tok_p = vp.on("vector",
                              lambda h, ft=ft, bank=bank_u: h.tensor_tensor(
                                  p_buf[:, ft * 512:(ft + 1) * 512], silb[:, :],
                                  bank[:, :], op=ALU.mult),
                              waits=[(s.act, tok_sil), (s.pe, tok_u)], inc=s.dve)
                hu_rot.set_free(bi_u, s.dve, tok_p)
            tok_pall = s.dve.count
            ysb = ysb_pair[bslot]
            for tt in range(4):
                for nh in range(2):
                    bank, fr, bi = y_rot.next()
                    for fc in range(n_ft):
                        vp.on("tensor",
                              lambda h, tt=tt, nh=nh, fc=fc, bank=bank, dT=dT: h.matmul(
                                  bank[:, :],
                                  p_buf[:, fc * 512 + tt * 128:fc * 512 + (tt + 1) * 128],
                                  dT[:, fc * 1024 + nh * 512:fc * 1024 + (nh + 1) * 512],
                                  start=(fc == 0), stop=(fc == n_ft - 1)),
                              waits=([(s.dve, tok_pall), fr,
                                      (s.dst, ysb_free[bslot])]
                                     if fc == 0 else ()),
                              inc=s.pe if fc == n_ft - 1 else None)
                    tok_y = s.pe.count
                    eng = "vector" if (tt * 2 + nh) % 2 == 0 else "scalar"
                    esem = s.dve if eng == "vector" else s.act
                    if comb_e is None:
                        def mk_cp(eng, tt=tt, nh=nh, bank=bank, ysb=ysb):
                            dst = ysb[:, tt * D + nh * 512:tt * D + (nh + 1) * 512]
                            if eng == "vector":
                                return lambda h: h.tensor_copy(dst, bank[:, :])
                            return lambda h: h.activation(dst, bank[:, :], AF.Copy)
                        tok_ev = vp.on(eng, mk_cp(eng),
                                       waits=[(s.pe, tok_y)], inc=esem)
                    else:
                        def mk_ev(eng, g=g, tt=tt, nh=nh, bank=bank, ysb=ysb):
                            cs = comb[:, (g * 4 + tt) * 16 + comb_e:
                                      (g * 4 + tt) * 16 + comb_e + 1]
                            dst = ysb[:, tt * D + nh * 512:tt * D + (nh + 1) * 512]
                            if eng == "vector":
                                return lambda h: h.tensor_scalar(dst, bank[:, :], cs,
                                                                 None, op0=ALU.mult)
                            return lambda h: h.activation(dst, bank[:, :], AF.Copy,
                                                          scale=cs)
                        tok_ev = vp.on(eng, mk_ev(eng),
                                       waits=[(s.pe, tok_y)], inc=esem)
                    y_rot.set_free(bi, esem, tok_ev)
            tok_st = vp.on("gpsimd",
                           lambda h, g=g, ysb=ysb, accum=accum: h.dma_start(
                               out=accr[g],
                               in_=ysb.ap().rearrange("p (tt d) -> p tt d", tt=4),
                               **({"accum_op": ALU.add} if accum else {})),
                           waits=[(s.dve, s.dve.count), (s.act, s.act.count),
                                  (s.dst, tok_accg[g])],
                           inc=s.dst, amt=16)
            ysb_free[bslot] = tok_st
            tok_accg[g] = tok_st
            if bg_step is not None:
                bg_step()

    with nc.sbuf_tensor("d_silb", [128, 512], BF16) as silb, \
         nc.sbuf_tensor("d_ysb1", [128, 4 * D], F32) as ysb1:
        with nc.sbuf_tensor("d_wstage", [128, 2048], F32) as wstage, \
             nc.sbuf_tensor("d_psh", [128, 16 * 512], BF16) as p_buf, \
             nc.sbuf_tensor("d_shg", [128, 8 * 2048], BF16) as shg, \
             nc.sbuf_tensor("d_shu", [128, 8 * 2048], BF16) as shu, \
             nc.sbuf_tensor("d_shd", [128, 16 * 1024], BF16) as shd:
            load_convert(vp, s, io["sh_gate_w"], 1024, 2048, wstage, shg)
            load_convert(vp, s, io["sh_up_w"], 1024, 2048, wstage, shu)
            load_convert(vp, s, io["sh_down_w"], 2048, 1024, wstage, shd)
            tok_w = s.dve.count
            vp.wait("tensor", s.dve, tok_comb)
            expert_ffn(shg, shu, shd, 16, 2048, None, p_buf, silb, [ysb1, ysb1],
                       tok_w, accum=False)
            vp.barrier(s.bar)

        with nc.sbuf_tensor("d_wstage2", [128, 1024], F32) as wstage, \
             nc.sbuf_tensor("d_psh2", [128, 8 * 512], BF16) as p_buf, \
             nc.sbuf_tensor("d_gweb0", [128, 8 * 1024], BF16) as gweb0, \
             nc.sbuf_tensor("d_uweb0", [128, 8 * 1024], BF16) as uweb0, \
             nc.sbuf_tensor("d_dweb0", [128, 8 * 1024], BF16) as dweb0, \
             nc.sbuf_tensor("d_gweb1", [128, 8 * 1024], BF16) as gweb1, \
             nc.sbuf_tensor("d_uweb1", [128, 8 * 1024], BF16) as uweb1, \
             nc.sbuf_tensor("d_dweb1", [128, 8 * 1024], BF16) as dweb1, \
             nc.sbuf_tensor("d_ysb2", [128, 4 * D], F32) as ysb2:
            wsets = [(gweb0, uweb0, dweb0), (gweb1, uweb1, dweb1)]
            wfree = [0, 0]
            wstages = [wstage, wstage]
            bgq = []

            def queue_expert(e):
                gw, uw, dw = wsets[e % 2]
                pg = lambda e=e: wfree[e % 2]
                for w_hbm, dst in ((io["gate_w"][e], gw), (io["up_w"][e], uw),
                                   (io["down_w"][e], dw)):
                    bgq.extend(load_convert_bg(vp, s, w_hbm, 1024, 1024,
                                               wstages, dst, pe_gate=pg))

            def bg_step():
                n = min(3, len(bgq))
                for f in bgq[:n]:
                    f()
                del bgq[:n]

            queue_expert(0)
            while bgq:
                bg_step()
            for e in range(E):
                if e + 1 < E:
                    queue_expert(e + 1)
                wready = s.pool.count
                expert_ffn(*wsets[e % 2], 8, 1024, e, p_buf, silb,
                           [ysb1, ysb2], wready, accum=True,
                           w_sem=s.pool, bg_step=bg_step)
                wfree[e % 2] = s.pe.count
        vp.barrier(s.bar)

    with nc.sbuf_tensor("d_sc", [128, 4 * D], F32) as sc, \
         nc.sbuf_tensor("d_accsb", [128, 4 * D], F32) as accsb, \
         nc.sbuf_tensor("d_sqb", [128, D], F32) as sqb:
        for tb in range(8):
            tok_ld = vp.on("sync",
                           lambda h, tb=tb: h.dma_start(
                               out=sc.ap().rearrange("p (tt d) -> p tt d", tt=4),
                               in_=s3r[tb]),
                           waits=[(s.dst, s.dst.count)],
                           inc=s.dld, amt=16)
            tok_lda = vp.on("sync",
                            lambda h, tb=tb: h.dma_start(
                                out=accsb.ap().rearrange("p (tt d) -> p tt d", tt=4),
                                in_=accr[tb]),
                            inc=s.dld, amt=16)
            vp.on("vector",
                  lambda h: h.tensor_tensor(sc[:, :], sc[:, :], accsb[:, :],
                                            op=ALU.add),
                  waits=[(s.dld, tok_lda)], inc=s.dve)
            for tt in range(4):
                tok_ln = layernorm_inplace(vp, s, sc[:, tt * D:(tt + 1) * D], sqb)
            vp.on("gpsimd",
                  lambda h, tb=tb: h.dma_start(
                      out=outr[tb],
                      in_=sc.ap().rearrange("p (tt d) -> p tt d", tt=4)),
                  waits=[(s.dve, tok_ln)], inc=s.dst, amt=16)

    vp.wait("gpsimd", s.dst, s.dst.count)
    return stk



def build_fused(nc, vp, io):
    gbar = vp.sem("gbar")
    stk = build_phase_a(nc, vp, io)
    vp.barrier(gbar)
    stk.close()
    stk = build_phase_b(nc, vp, io)
    vp.barrier(gbar)
    stk.close()
    stk = build_phase_c(nc, vp, io)
    vp.barrier(gbar)
    stk.close()
    stk = build_phase_d(nc, vp, io)
    return stk



def _dram_in(nc, name, shape):
    return nc.dram_tensor(name, list(shape), F32, kind="ExternalInput").ap()


def _run_phase(build, io_spec, in_maps, out_key):
    nc = bass.Bass("TRN2", target_bir_lowering=False)
    io = {}
    for key, kind, shape in io_spec:
        io[key] = nc.dram_tensor("t_" + key, list(shape), F32, kind=kind).ap()
    vp = VP(nc)
    stk = build(nc, vp, io)
    with stk:
        vp.run()
    res = run_bass_kernel_spmd(nc, in_maps, core_ids=list(range(8)))
    return [np.asarray(r["t_" + out_key]) for r in res.results]


def kernel(**inputs):
    g = lambda k: np.ascontiguousarray(np.asarray(inputs[k], dtype=np.float32))
    x = g("L0_attn_x")
    gate = g("L0_ffn_gate").reshape(1, 1)

    ins = [("x", (T, D)),
           ("Wq", (D, D)), ("bq", (D,)), ("Wk", (D, D)), ("bk", (D,)),
           ("Wv", (D, D)), ("bv", (D,)), ("Wo", (D, D)), ("bo", (D,)),
           ("w_fc", (4096, D)), ("w_proj", (D, 4096)), ("u", (4096, 16)),
           ("v", (16, D)), ("fast_b", (16, 16)), ("gate", (1, 1)),
           ("Wqkv", (D, 3 * D)), ("Wout", (D, D)), ("head_scales", (8,)),
           ("head_dirs", (8, 3)),
           ("router_w", (16, D)), ("bias", (16,)), ("gate_w", (16, D, D)),
           ("up_w", (16, D, D)), ("down_w", (16, D, D)),
           ("sh_gate_w", (D, 2048)), ("sh_up_w", (D, 2048)),
           ("sh_down_w", (2048, D))]

    nc = bass.Bass("TRN2", target_bir_lowering=False)
    io = {}
    for key, shape in ins:
        io[key] = nc.dram_tensor("t_" + key, list(shape), F32,
                                 kind="ExternalInput").ap()
    for key in ("s1", "s2", "s3"):
        io[key] = nc.dram_tensor("t_" + key, [T, D], F32, kind="Internal").ap()
    io["out"] = nc.dram_tensor("t_out", [T, D], F32, kind="ExternalOutput").ap()

    vp = VP(nc)
    stk = build_fused(nc, vp, io)
    with stk:
        vp.run()

    com = dict(t_Wq=g("L0_attn_Wq"), t_bq=g("L0_attn_bq"),
               t_Wk=g("L0_attn_Wk"), t_bk=g("L0_attn_bk"),
               t_Wv=g("L0_attn_Wv"), t_bv=g("L0_attn_bv"),
               t_Wo=g("L0_attn_Wo"), t_bo=g("L0_attn_bo"),
               t_w_fc=g("L0_ffn_w_fc"), t_w_proj=g("L0_ffn_w_proj"),
               t_u=g("L0_ffn_u"), t_v=g("L0_ffn_v"),
               t_fast_b=g("L0_ffn_fast_b"), t_gate=gate,
               t_Wqkv=g("L1_attn_Wqkv"), t_Wout=g("L1_attn_Wout"),
               t_head_scales=g("L1_attn_head_scales"),
               t_head_dirs=g("L1_attn_head_dirs"),
               t_router_w=g("L1_ffn_router_w"), t_bias=g("L1_ffn_bias"),
               t_gate_w=g("L1_ffn_gate_w"), t_up_w=g("L1_ffn_up_w"),
               t_down_w=g("L1_ffn_down_w"), t_sh_gate_w=g("L1_ffn_sh_gate_w"),
               t_sh_up_w=g("L1_ffn_sh_up_w"), t_sh_down_w=g("L1_ffn_sh_down_w"))
    maps = [dict(t_x=x[b], **com) for b in range(8)]
    res = run_bass_kernel_spmd(nc, maps, core_ids=list(range(8)))
    return np.stack([np.asarray(r["t_out"]) for r in res.results],
                    axis=0).astype(np.float32)
